# version 04
# baseline (speedup 1.0000x reference)
import math
from contextlib import ExitStack

import numpy as np
import ml_dtypes
import concourse.bass as bass
import concourse.mybir as mybir
from concourse.bass_utils import run_bass_kernel_spmd

F32 = mybir.dt.float32
BF16 = mybir.dt.bfloat16
AF = mybir.ActivationFunctionType
ALU = mybir.AluOpType
AX = mybir.AxisListType

S, D, H, DH, E, F = 4096, 1024, 16, 64, 16, 4096
NTT = S // 128
NDC = D // 128
NTOUT = 4
TKT = S // NTOUT
CSC = float(math.sqrt(2 * DH))
WTS = 8064
MAGIC = 12582912.0
INV2PI = 1.0 / (2.0 * math.pi)
TWOPI = 2.0 * math.pi

DEBUG = False
STAGES = "all"

ENGS = ("tensor", "vector", "scalar", "sync", "gpsimd")


class Sched:

    def __init__(self, semh):
        self.ops = {e: [] for e in ENGS}
        self.cnt = {s: 0 for s in semh}
        self.semh = semh
        self.last_wait = {e: {} for e in ENGS}

    def do(self, eng, emit, wait=(), inc=()):
        inc = tuple(inc)
        for s, k in inc:
            self.cnt[s] += k
        self.ops[eng].append((emit, tuple(w for w in wait if w is not None), inc))
        return {s: (s, self.cnt[s]) for s, _ in inc}

    def tok(self, sem):
        return (sem, self.cnt[sem])

    def barrier(self, exclude=()):
        waits = [self.tok(s) for s in self.cnt if s not in exclude]
        for eng in ENGS:
            self.do(eng, lambda e: e.nop(), waits, ())

    def flush(self, nc):
        def mk_body(eng):
            ops = self.ops[eng]
            lw = self.last_wait[eng]

            def body(proxy):
                for emit, waits, incs in ops:
                    for s, v in waits:
                        if lw.get(s, -1) < v:
                            proxy.wait_ge(self.semh[s], v)
                            lw[s] = v
                    ins = emit(proxy)
                    for s, k in incs:
                        ins.then_inc(self.semh[s], k)
            return body

        with nc.Block() as block:
            for eng in ENGS:
                if self.ops[eng]:
                    getattr(block, eng)(mk_body(eng))
        self.ops = {e: [] for e in ENGS}


def build_nc(debug=False, stages='all'):
    nc = bass.Bass("TRN2", target_bir_lowering=False)

    x_d = nc.dram_tensor("x", [S, D], F32, kind="ExternalInput").ap()
    gamma_d = nc.dram_tensor("gamma", [1, D], F32, kind="ExternalInput").ap()
    beta_d = nc.dram_tensor("beta", [1, D], F32, kind="ExternalInput").ap()
    wpos_d = nc.dram_tensor("w_pos", [2 * D, D], F32, kind="ExternalInput").ap()
    wtok_d = nc.dram_tensor("w_tok", [3 * D, D], F32, kind="ExternalInput").ap()
    btab_d = nc.dram_tensor("bias_table", [2048, H], F32, kind="ExternalInput").ap()
    keys_d = nc.dram_tensor("keys", [E, D], F32, kind="ExternalInput").ap()
    rw_d = nc.dram_tensor("rw", [E, D], F32, kind="ExternalInput").ap()
    rb_d = nc.dram_tensor("rb", [E, 1], F32, kind="ExternalInput").ap()
    w1_d = nc.dram_tensor("w1", [E, F, D], F32, kind="ExternalInput").ap()
    b1_d = nc.dram_tensor("b1", [E, F], F32, kind="ExternalInput").ap()
    w2_d = nc.dram_tensor("w2", [E, D, F], F32, kind="ExternalInput").ap()
    b2_d = nc.dram_tensor("b2", [E, D], F32, kind="ExternalInput").ap()
    id16f_d = nc.dram_tensor("id16f", [16, 16], F32, kind="ExternalInput").ap()
    id16b_d = nc.dram_tensor("id16b", [16, 16], BF16, kind="ExternalInput").ap()
    id128f_d = nc.dram_tensor("id128f", [128, 128], F32, kind="ExternalInput").ap()
    id128b_d = nc.dram_tensor("id128b", [128, 128], BF16, kind="ExternalInput").ap()
    y_d = nc.dram_tensor("y", [S, D], F32, kind="ExternalOutput").ap()

    w1b_d = nc.dram_tensor("w1b", [E, F, D], BF16).ap()
    w2b_d = nc.dram_tensor("w2b", [E, D, F], BF16).ap()
    w1tb_d = nc.dram_tensor("w1tb", [E, 8, 128, F], BF16).ap()
    w2tb_d = nc.dram_tensor("w2tb", [E, 32, 128, D], BF16).ap()
    wtokb_d = nc.dram_tensor("wtokb", [3 * D, D], BF16).ap()
    wposb_d = nc.dram_tensor("wposb", [2 * D, D], BF16).ap()
    xnh_d = nc.dram_tensor("xnh", [S, D], BF16).ap()
    xn2h_d = nc.dram_tensor("xn2h", [S, D], BF16).ap()
    peh_d = nc.dram_tensor("peh", [S, D], BF16).ap()
    stkK_d = nc.dram_tensor("stkK", [H, 128, S], BF16).ap()
    stkQ_d = nc.dram_tensor("stkQ", [H, 128, S], BF16).ap()
    vTh_d = nc.dram_tensor("vTh", [D, S], BF16).ap()
    vtm_d = nc.dram_tensor("vtm", [S, D], BF16).ap()
    texth_d = nc.dram_tensor("texth", [H, 8192], BF16).ap()
    sh_d = nc.dram_tensor("sh", [S, D], BF16).ap()
    mTh_d = nc.dram_tensor("mTh", [16, S], BF16).ap()
    gTh_d = nc.dram_tensor("gTh", [D, S], BF16).ap()

    dbg = {}
    if debug:
        for name, shp, dt in [
            ("dbg_xn", [S, D], BF16), ("dbg_xn2", [S, D], BF16),
            ("dbg_pe", [S, D], BF16), ("dbg_stkK", [H, 128, S], BF16),
            ("dbg_stkQ", [H, 128, S], BF16), ("dbg_v", [S, D], BF16),
            ("dbg_text", [H, 8192], BF16), ("dbg_s", [S, D], BF16),
            ("dbg_g", [D, S], BF16), ("dbg_mT", [16, S], BF16),
        ]:
            dbg[name] = nc.dram_tensor(name, shp, dt, kind="ExternalOutput").ap()

    ctx = ExitStack()
    semh = {}
    for s in ("dS", "dA", "dG", "tM", "vV", "sS", "gG"):
        semh[s] = ctx.enter_context(nc.semaphore(s))
    Sch = Sched(semh)

    def DMA(q, dst, src, wait=()):
        sem = {"sync": "dS", "scalar": "dA", "gpsimd": "dG"}[q]
        return Sch.do(q, lambda e, a=dst, b=src: e.dma_start(a, b), wait,
                      ((sem, 16),))[sem]

    def DMAT(dst, src, wait=()):
        return Sch.do("sync", lambda e, a=dst, b=src: e.dma_start_transpose(a, b),
                      wait, (("dS", 16),))["dS"]

    def TEN(emit, wait=(), inc=1):
        return Sch.do("tensor", emit, wait, ((("tM", inc),) if inc else ()))

    def VEC(emit, wait=(), inc=1):
        return Sch.do("vector", emit, wait, ((("vV", inc),) if inc else ()))

    def SCA(emit, wait=(), inc=1):
        return Sch.do("scalar", emit, wait, ((("sS", inc),) if inc else ()))

    def sb(name, shape, dt, stack=None, **kw):
        return (stack or ctx).enter_context(nc.sbuf_tensor(name, shape, dt, **kw))

    def vgap(n=2):
        for _ in range(n):
            Sch.do("vector", lambda e: e.tensor_scalar_mul(junk[:, 0:1],
                                                           junk[:, 0:1], 1.0),
                   (), ())

    def ps(name, shape, stack=None):
        return (stack or ctx).enter_context(nc.psum_tensor(name, shape, F32))

    b1T = sb("b1T", [128, 32 * 16], F32)
    b2b = sb("b2b", [16, D], BF16)
    mT = sb("mT", [16, S], BF16)
    onesb = sb("onesb", [1, 128], BF16)
    onesf = sb("onesf", [1, 128], F32)
    epsc = sb("epsc", [128, 1], F32)
    junk = sb("junk", [128, 2], F32)

    X = {}

    with ExitStack() as p0:
        pA0 = ps("pA0", [128, 1024], p0)
        pB0a = ps("pB0a", [128, 512], p0)
        pB0b = ps("pB0b", [128, 512], p0)
        pC0 = ps("pC0", [128, 128], p0)
        pD0 = ps("pD0", [128, 512], p0)

        gbc = sb("gbc", [128, D], F32, p0)
        bbc = sb("bbc", [128, D], F32, p0)
        cw = [sb("cw0", [128, 8192], BF16, p0), sb("cw1", [128, 8192], BF16, p0)]
        cbc = sb("cbc", [128, 16], F32, p0)
        atT = sb("atT", [128, 128], BF16, p0)
        id128b = sb("s_id128b", [128, 128], BF16, p0)
        pa = ExitStack()
        id16f = sb("s_id16f", [16, 16], F32, pa)
        id16b = sb("s_id16b", [16, 16], BF16, pa)
        id128f = sb("s_id128f", [128, 128], F32, pa)
        grow = sb("grow", [1, D], F32, pa)
        brow = sb("brow", [1, D], F32, pa)

        t_id16f = DMA("scalar", id16f[:, :], id16f_d[:, :])
        t_id16b = DMA("scalar", id16b[:, :], id16b_d[:, :])
        t_id128f = DMA("scalar", id128f[:, :], id128f_d[:, :])
        t_id128b = DMA("scalar", id128b[:, :], id128b_d[:, :])
        t_grow = DMA("scalar", grow[:, :], gamma_d[:, :])
        t_brow = DMA("scalar", brow[:, :], beta_d[:, :])
        VEC(lambda e: e.memset(onesb[:, :], 1.0))
        VEC(lambda e: e.memset(junk[:, :], 1.0))
        VEC(lambda e: e.memset(epsc[:, :], 1e-5))
        t_ones = VEC(lambda e: e.memset(onesf[:, :], 1.0))["vV"]

        for half in range(2):
            TEN(lambda e, h=half: e.matmul(pA0[:, h * 512:(h + 1) * 512], onesf[:, :],
                                           grow[0:1, h * 512:(h + 1) * 512],
                                           start=True, stop=True),
                [t_grow, t_ones], inc=(1 if half == 1 else 0))
        t_gbc = VEC(lambda e: e.tensor_copy(gbc[:, :], pA0[:, :]),
                    [Sch.tok("tM")])["vV"]
        for half in range(2):
            TEN(lambda e, h=half: e.matmul(pA0[:, h * 512:(h + 1) * 512], onesf[:, :],
                                           brow[0:1, h * 512:(h + 1) * 512],
                                           start=True, stop=True),
                [t_brow, ("vV", t_gbc[1])], inc=(1 if half == 1 else 0))
        t_bbc = VEC(lambda e: e.tensor_copy(bbc[:, :], pA0[:, :]),
                    [Sch.tok("tM")])["vV"]

        cw_free = [None, None]
        cast_done = {}
        cast_jobs = []
        for c in range(3):
            cast_jobs.append(("wtok", wtok_d, wtokb_d, c * 8 * 128 * D, 8, D))
        for c in range(2):
            cast_jobs.append(("wpos", wpos_d, wposb_d, c * 8 * 128 * D, 8, D))
        for e_i in range(E):
            for c in range(4):
                cast_jobs.append((f"w1_{e_i}", w1_d, w1b_d,
                                  e_i * F * D + c * 8 * 128 * D, 8, D))
            for c in range(4):
                cast_jobs.append((f"w2_{e_i}", w2_d, w2b_d,
                                  e_i * D * F + c * 2 * 128 * F, 2, F))
        for j, (tag, src_d, dst_d, off, nblk, inner) in enumerate(cast_jobs):
            buf = cw[j % 2]
            src_ap = bass.AP(src_d.tensor, off,
                             [[inner, 128], [128 * inner, nblk], [1, inner]])
            dst_ap = bass.AP(dst_d.tensor, off,
                             [[inner, 128], [128 * inner, nblk], [1, inner]])
            bview = buf[:, 0:nblk * inner].rearrange("p (b i) -> p b i", b=nblk)
            w = [cw_free[j % 2]] if cw_free[j % 2] else []
            t_in = DMA("gpsimd", bview, src_ap, w)
            t_out = DMA("gpsimd", dst_ap, bview, [t_in])
            cw_free[j % 2] = t_out
            if tag == "wtok":
                t_wtok_cast = t_out
            elif tag == "wpos":
                t_wpos_cast = t_out
            elif tag.startswith("w2_"):
                cast_done[int(tag.split("_")[1])] = t_out
        X["cast_done"] = cast_done

        btn = sb("btn", [128, 256], F32, pa)
        btT = sb("btT", [16, 2048], F32, pa)
        arr = sb("arr", [16, 8192], F32, pa)
        arrb = sb("arrb", [16, 8192], BF16, pa)
        t_btn = None
        for blk in range(16):
            t_btn = DMA("scalar", btn[:, blk * 16:(blk + 1) * 16],
                        btab_d[blk * 128:(blk + 1) * 128, :])
        t_cp = None
        for blk in range(16):
            w = [t_btn, t_id128f] + ([t_cp] if t_cp else [])
            TEN(lambda e, b=blk: e.matmul(pC0[0:16, :], btn[:, b * 16:(b + 1) * 16],
                                          id128f[:, :], start=True, stop=True), w)
            t_cp = VEC(lambda e, b=blk: e.tensor_copy(btT[:, b * 128:(b + 1) * 128],
                                                      pC0[0:16, :]),
                       [Sch.tok("tM")])["vV"]
        VEC(lambda e: e.memset(arr[:, :], 1.0))
        t_btsc = VEC(lambda e: e.tensor_scalar_mul(btT[:, :], btT[:, :],
                                                   1.0 / CSC))["vV"]
        VEC(lambda e: e.tensor_copy(arr[:, 3072:5120], btT[:, ::-1]), [t_btsc])
        VEC(lambda e: e.tensor_copy(arr[:, 5120:7168], btT[:, ::-1]))
        VEC(lambda e: e.tensor_scalar_mul(arr[:, 0:3072], arr[:, 0:3072],
                                          btT[:, 2047:2048]), [t_btsc])
        VEC(lambda e: e.tensor_scalar_mul(arr[:, 7168:8192], arr[:, 7168:8192],
                                          btT[:, 0:1]))
        t_arr = VEC(lambda e: e.tensor_copy(arrb[:, :], arr[:, :]))["vV"]
        X["t_text"] = DMA("scalar", texth_d[:, :], arrb[:, :], [t_arr])

        keysb = sb("keysb", [16, D], F32, pa)
        rwb = sb("rwb", [16, D], F32, pa)
        Af = sb("Af", [16, D], F32, pa)
        Abf = sb("Abf", [16, D], BF16, pa)
        k2 = sb("k2", [16, D], F32, pa)
        kcol = sb("kcol", [16, 1], F32, pa)
        rbcol = sb("rbcol", [16, 1], F32, pa)
        ccol = sb("ccol", [16, 1], F32, pa)
        crow = sb("crow", [1, 16], F32, pa)
        t_keys = DMA("scalar", keysb[:, :], keys_d[:, :])
        t_rw = DMA("scalar", rwb[:, :], rw_d[:, :])
        t_rb = DMA("scalar", rbcol[:, :], rb_d[:, :])
        t_af1 = VEC(lambda e: e.tensor_scalar_mul(Af[:, :], keysb[:, :], 2.0),
                    [t_keys])["vV"]
        vgap()
        t_af2 = VEC(lambda e: e.tensor_add(Af[:, :], Af[:, :], rwb[:, :]),
                    [t_rw, t_af1])["vV"]
        vgap()
        t_Abf = VEC(lambda e: e.tensor_copy(Abf[:, :], Af[:, :]), [t_af2])["vV"]
        t_k2 = SCA(lambda e: e.activation(k2[:, :], keysb[:, :], AF.Square),
                   [t_keys])["sS"]
        t_kc = VEC(lambda e: e.reduce_sum(kcol[:, :], k2[:, :], axis=AX.X),
                   [t_k2])["vV"]
        vgap()
        t_ccol = VEC(lambda e: e.tensor_sub(ccol[:, :], rbcol[:, :], kcol[:, :]),
                     [t_rb, t_kc])["vV"]
        TEN(lambda e: e.matmul(pC0[0:1, 0:16], ccol[:, :], id16f[:, :],
                               start=True, stop=True),
            [t_ccol, t_id16f, t_cp])
        t_crow = VEC(lambda e: e.tensor_copy(crow[:, :], pC0[0:1, 0:16]),
                     [Sch.tok("tM")])["vV"]
        TEN(lambda e: e.matmul(pC0[:, 0:16], onesf[:, :], crow[:, :],
                               start=True, stop=True), [t_crow])
        t_cbc = VEC(lambda e: e.tensor_copy(cbc[:, :], pC0[:, 0:16]),
                    [Sch.tok("tM")])["vV"]
        t_at = t_cbc
        for dc in range(NDC):
            TEN(lambda e, d=dc: e.matmul(pC0[:, 0:16], Abf[:, d * 128:(d + 1) * 128],
                                         id16b[:, :], start=True, stop=True),
                [t_Abf, t_id16b, t_at])
            t_at = VEC(lambda e, d=dc: e.tensor_copy(atT[:, d * 16:(d + 1) * 16],
                                                     pC0[:, 0:16]),
                       [Sch.tok("tM")])["vV"]
        t_atT = t_at

        b1sb = sb("b1sb", [16, F], F32, pa)
        t_b1 = DMA("scalar", b1sb[:, :], b1_d[:, :])
        t_bb = t_atT
        for ft in range(32):
            TEN(lambda e, f=ft: e.matmul(pC0[:, 0:16], b1sb[:, f * 128:(f + 1) * 128],
                                         id16f[:, :], start=True, stop=True),
                [t_b1, t_id16f, t_bb])
            t_bb = VEC(lambda e, f=ft: e.tensor_copy(b1T[:, f * 16:(f + 1) * 16],
                                                     pC0[:, 0:16]),
                       [Sch.tok("tM")])["vV"]
        X["t_b1T"] = t_bb
        X["t_b2"] = DMA("gpsimd", b2b[:, :], b2_d[:, :])

        Sch.barrier(exclude=("dG",))
        Sch.flush(nc)
        pa.close()

        pb = ExitStack()
        iota = sb("iota", [1, S], F32, pb)
        divr = sb("divr", [1, 512], F32, pb)
        Sch.do("gpsimd", lambda e: e.iota(iota[:, :], [[1, S]], channel_multiplier=0,
                                          allow_small_or_imprecise_dtypes=True),
               (), (("gG", 1),))
        t_iota = Sch.tok("gG")
        t_div = SCA(lambda e: e.activation(divr[:, :], iota[0:1, 0:512], AF.Exp,
                                           scale=-math.log(10000.0) / 512.0),
                    [t_iota])["sS"]
        pet = [sb("pet0", [128, D], BF16, pb), sb("pet1", [128, D], BF16, pb)]
        rr = [sb(f"rr{i}", [128, 512], F32, pb) for i in range(3)]
        pet_free = [None, None]
        t_sin_prev = None
        t_r1_prev = None
        for tt in range(NTT):
            buf = pet[tt % 2]
            w = [t_div, t_iota] + ([t_r1_prev] if t_r1_prev else [])
            TEN(lambda e, t=tt: e.matmul(pD0[:, :], iota[0:1, t * 128:(t + 1) * 128],
                                         divr[:, :], start=True, stop=True), w)
            t_ang = Sch.tok("tM")
            war = [t_sin_prev] if tt > 0 else []
            t_r1_prev = VEC(lambda e: e.tensor_scalar_mul(rr[0][:, :], pD0[:, :],
                                                          INV2PI), [t_ang] + war)["vV"]
            VEC(lambda e: e.tensor_scalar_add(rr[1][:, :], rr[0][:, :], MAGIC))
            VEC(lambda e: e.tensor_scalar_add(rr[1][:, :], rr[1][:, :], -MAGIC))
            t_r4 = VEC(lambda e: e.tensor_sub(rr[1][:, :], rr[0][:, :],
                                              rr[1][:, :]))["vV"]
            w_buf = [pet_free[tt % 2]] if pet_free[tt % 2] else []
            sin_view = buf[:, :].rearrange("p (i two) -> p two i", two=2)
            SCA(lambda e, a=sin_view: e.activation(a[:, 0:1, :], rr[1][:, :], AF.Sin,
                                                   scale=TWOPI), [t_r4] + w_buf)
            VEC(lambda e: e.tensor_scalar_add(rr[2][:, :], rr[1][:, :], 0.25))
            VEC(lambda e: e.tensor_scalar_add(rr[0][:, :], rr[2][:, :], MAGIC))
            VEC(lambda e: e.tensor_scalar_add(rr[0][:, :], rr[0][:, :], -MAGIC))
            t_r8 = VEC(lambda e: e.tensor_sub(rr[2][:, :], rr[2][:, :],
                                              rr[0][:, :]))["vV"]
            t_sin = SCA(lambda e, a=sin_view: e.activation(a[:, 1:2, :], rr[2][:, :],
                                                           AF.Sin, scale=TWOPI),
                        [t_r8])["sS"]
            t_sin_prev = t_sin
            pet_free[tt % 2] = DMA("sync", peh_d[tt * 128:(tt + 1) * 128, :],
                                   buf[:, :], [t_sin])
        t_peh = pet_free[(NTT - 1) % 2]

        xt = [sb("xt0", [128, D], F32, pb), sb("xt1", [128, D], F32, pb)]
        xsq = [sb("xsq0", [128, D], F32, pb), sb("xsq1", [128, D], F32, pb)]
        xnf = sb("xnf", [128, D], F32, pb)
        xn2f = sb("xn2f", [128, D], F32, pb)
        xnb = sb("xnb", [128, D], BF16, pb)
        xn2b = sb("xn2b", [128, D], BF16, pb)
        cc = [[sb(f"cc{l}_{i}", [128, 1], F32, pb) for i in range(6)]
              for l in range(2)]
        ln_war = [None, None]
        ln_war2 = [None, None]

        def ln(lvl, inp, outp, t_in):
            c = cc[lvl]
            q = xsq[lvl]
            w2_ = [ln_war2[lvl]] if ln_war2[lvl] else []
            t_sq = SCA(lambda e: e.activation(q[:, :], inp[:, :], AF.Square,
                                              accum_out=c[1][:, :]),
                       [t_in] + w2_)["sS"]
            t_r0 = VEC(lambda e: e.reduce_sum(c[0][:, :], inp[:, :], axis=AX.X),
                       [t_in])["vV"]
            vgap()
            t_mu = VEC(lambda e: e.tensor_scalar_mul(c[0][:, :], c[0][:, :],
                                                     1.0 / D), [t_r0])["vV"]
            VEC(lambda e: e.tensor_scalar_mul(c[1][:, :], c[1][:, :], 1.0 / D),
                [t_sq])
            vgap()
            t_c2 = VEC(lambda e: e.tensor_mul(c[2][:, :], c[0][:, :], c[0][:, :]),
                       [t_mu])["vV"]
            vgap()
            t_var = VEC(lambda e: e.tensor_sub(c[1][:, :], c[1][:, :],
                                               c[2][:, :]), [t_c2])["vV"]
            SCA(lambda e: e.activation(c[3][:, :], c[1][:, :], AF.Sqrt,
                                       bias=epsc[:, 0:1]), [t_var])
            t_sd = Sch.tok("sS")
            ln_war2[lvl] = t_sd
            t_rc = VEC(lambda e: e.reciprocal(c[4][:, :], c[3][:, :]), [t_sd])["vV"]
            vgap()
            w_out = [ln_war[lvl]] if ln_war[lvl] else []
            VEC(lambda e: e.tensor_scalar_sub(outp[:, :], inp[:, :], c[0][:, 0:1]),
                [t_mu] + w_out)
            t_o = VEC(lambda e: e.tensor_scalar_mul(outp[:, :], outp[:, :],
                                                    c[4][:, 0:1]), [t_rc])["vV"]
            ln_war[lvl] = None
            return t_o, t_sq

        xt_free = [None, None]
        xnb_free = None
        xn2b_free = None
        prev_sq2 = None
        for tt in range(NTT):
            buf = xt[tt % 2]
            w = list(xt_free[tt % 2]) if xt_free[tt % 2] else []
            t_x = DMA("sync", buf[:, :], x_d[tt * 128:(tt + 1) * 128, :], w)
            ln_war[0] = prev_sq2
            t_xn, t_sq1 = ln(0, buf, xnf, t_x)
            xt_free[tt % 2] = (t_xn, t_sq1)
            t_xn2n, t_sq2 = ln(1, xnf, xn2f, t_xn)
            prev_sq2 = t_sq2
            VEC(lambda e: e.tensor_mul(xn2f[:, :], xn2f[:, :], gbc[:, :]),
                [t_xn2n, t_gbc])
            t_x2 = VEC(lambda e: e.tensor_add(xn2f[:, :], xn2f[:, :], bbc[:, :]),
                       [t_bbc])["vV"]
            wb1 = [xnb_free] if xnb_free else []
            t_c1 = VEC(lambda e: e.tensor_copy(xnb[:, :], xnf[:, :]),
                       [t_xn2n] + wb1)["vV"]
            xnb_free = DMA("sync", xnh_d[tt * 128:(tt + 1) * 128, :], xnb[:, :],
                           [t_c1])
            wb2 = [xn2b_free] if xn2b_free else []
            t_c2 = VEC(lambda e: e.tensor_copy(xn2b[:, :], xn2f[:, :]),
                       [t_x2] + wb2)["vV"]
            xn2b_free = DMA("sync", xn2h_d[tt * 128:(tt + 1) * 128, :], xn2b[:, :],
                            [t_c2])
        t_xnh = xnb_free
        t_xn2h = xn2b_free

        Sch.barrier(exclude=("dG",))
        Sch.flush(nc)
        pb.close()

        pc2 = ExitStack()
        slabX = sb("slabX", [128, 8 * S], BF16, pc2, is_transpose_dest=True)
        slabW = sb("slabW", [128, 8 * 3072], BF16, pc2, is_transpose_dest=True)
        ev = [sb("ev0", [128, S], BF16, pc2), sb("ev1", [128, S], BF16, pc2)]
        vt = [sb("vt0", [128, D], BF16, pc2, is_transpose_dest=True),
              sb("vt1", [128, D], BF16, pc2, is_transpose_dest=True)]

        t_peT = None
        for dc in range(NDC):
            t_peT = DMAT(slabX[:, dc * S:(dc + 1) * S],
                         peh_d[0:S, dc * 128:(dc + 1) * 128], [t_peh])
        t_wposT = None
        for dc in range(NDC):
            t_wposT = DMAT(slabW[:, dc * 2048:(dc + 1) * 2048],
                           wposb_d[0:2048, dc * 128:(dc + 1) * 128], [t_wpos_cast])

        ev_free = [None, None]
        pb_free = [None, None]

        def proj_block(n_feat_tiles, wwid, t_w, t_x, dest_fn):
            for pf in range(n_feat_tiles):
                ebuf = ev[pf % 2]
                efw = [ev_free[pf % 2]] if ev_free[pf % 2] else []
                for tc2 in range(8):
                    pb = pB0a if tc2 % 2 == 0 else pB0b
                    pbw = [pb_free[tc2 % 2]] if pb_free[tc2 % 2] else []
                    for dc in range(NDC):
                        TEN(lambda e, p=pf, t=tc2, d=dc, q=pb:
                            e.matmul(q[:, :],
                                     slabW[:, d * wwid + p * 128:
                                           d * wwid + (p + 1) * 128],
                                     slabX[:, d * S + t * 512:d * S + (t + 1) * 512],
                                     start=(d == 0), stop=(d == NDC - 1)),
                            ([t_w, t_x] + pbw) if dc == 0 else (),
                            inc=(1 if (dc == NDC - 1) else 0))
                    t_mm = Sch.tok("tM")
                    pb_free[tc2 % 2] = VEC(
                        lambda e, t=tc2, b=ebuf, q=pb:
                        e.tensor_copy(b[:, t * 512:(t + 1) * 512], q[:, :]),
                        [t_mm] + (efw if tc2 == 0 else []))["vV"]
                t_ev = Sch.tok("vV")
                ev_free[pf % 2] = dest_fn(pf, ebuf, t_ev)

        def pos_dest(pf, ebuf, t_ev):
            dst = stkK_d if pf < 8 else stkQ_d
            h0 = 2 * (pf % 8)
            DMA("sync", dst[h0, 64:128, :], ebuf[0:64, :], [t_ev])
            return DMA("sync", dst[h0 + 1, 64:128, :], ebuf[64:128, :], [t_ev])

        proj_block(16, 2048, t_wposT, t_peT, pos_dest)

        t_posmm = Sch.tok("tM")
        t_xn2T = None
        for dc in range(NDC):
            t_xn2T = DMAT(slabX[:, dc * S:(dc + 1) * S],
                          xn2h_d[0:S, dc * 128:(dc + 1) * 128],
                          [t_xn2h, t_posmm])
        t_wtokT = None
        for dc in range(NDC):
            t_wtokT = DMAT(slabW[:, dc * 3072:(dc + 1) * 3072],
                           wtokb_d[0:3072, dc * 128:(dc + 1) * 128],
                           [t_wtok_cast, t_posmm])

        def tok_dest(tf, ebuf, t_ev):
            if tf < 16:
                dst = stkK_d if tf < 8 else stkQ_d
                h0 = 2 * (tf % 8)
                DMA("sync", dst[h0, 0:64, :], ebuf[0:64, :], [t_ev])
                return DMA("sync", dst[h0 + 1, 0:64, :], ebuf[64:128, :], [t_ev])
            return DMA("sync", vTh_d[(tf - 16) * 128:(tf - 15) * 128, :],
                       ebuf[:, :], [t_ev])

        proj_block(24, 3072, t_wtokT, t_xn2T, tok_dest)
        t_vTh = ev_free[23 % 2]

        vt_free = [None, None]
        for tt in range(NTT):
            buf = vt[tt % 2]
            w = [t_vTh] + ([vt_free[tt % 2]] if vt_free[tt % 2] else [])
            t_tr = DMAT(buf[:, :], vTh_d[0:D, tt * 128:(tt + 1) * 128], w)
            vt_free[tt % 2] = DMA("sync", vtm_d[tt * 128:(tt + 1) * 128, :],
                                  buf[:, :], [t_tr])
        X["t_vtm"] = vt_free[(NTT - 1) % 2]

        wst = [sb("wst0", [128, F], BF16, pc2, is_transpose_dest=True),
               sb("wst1", [128, F], BF16, pc2, is_transpose_dest=True)]
        wst_free = [None, None]
        repack_done = {}
        for e_i in range(E):
            for dc in range(NDC):
                i = (e_i * NDC + dc) % 2
                w = [X["cast_done"][e_i]] + ([wst_free[i]] if wst_free[i] else [])
                t_t = DMAT(wst[i][:, :],
                           bass.AP(w1b_d.tensor, e_i * F * D + dc * 128,
                                   [[D, F], [1, 128]]), w)
                wst_free[i] = DMA("scalar", w1tb_d[e_i, dc, :, :], wst[i][:, :],
                                  [t_t])
        wst2 = [sb("wst2a", [128, D], BF16, pc2, is_transpose_dest=True),
                sb("wst2b", [128, D], BF16, pc2, is_transpose_dest=True)]
        wst2_free = [None, None]
        for e_i in range(E):
            for fc in range(32):
                i = (e_i * 32 + fc) % 2
                w = [X["cast_done"][e_i]] + ([wst2_free[i]] if wst2_free[i] else [])
                t_t = DMAT(wst2[i][:, :],
                           bass.AP(w2b_d.tensor, e_i * D * F + fc * 128,
                                   [[F, D], [1, 128]]), w)
                wst2_free[i] = DMA("scalar", w2tb_d[e_i, fc, :, :], wst2[i][:, :],
                                   [t_t])
            repack_done[e_i] = wst2_free[(e_i * 32 + 31) % 2]
        X["repack_done"] = repack_done



        t_qkvmm = Sch.tok("tM")
        t_xnT = None
        for dc in range(NDC):
            t_xnT = DMAT(slabX[:, dc * S:(dc + 1) * S],
                         xnh_d[0:S, dc * 128:(dc + 1) * 128], [t_xnh, t_qkvmm])
        Lsb = sb("Lsb", [128, 16], F32, pc2)
        t8 = sb("t8", [128, 8], F32, pc2)
        nv1 = sb("nv1", [128, 1], F32, pc2)
        Ue = sb("Ue", [128, 16], F32, pc2)
        gemask = sb("gemask", [128, 16], F32, pc2)
        rsum = sb("rsum", [128, 1], F32, pc2)
        rrec = sb("rrec", [128, 1], F32, pc2)
        mbf = sb("mbf", [128, 16], BF16, pc2)
        pbl_free = [None, None]
        t_mtprev = t_cbc
        prev_lexp = None
        prev_mtmm = None
        for tt in range(NTT):
            pb = pB0a if tt % 2 == 0 else pB0b
            pbw = [pbl_free[tt % 2]] if pbl_free[tt % 2] else [Sch.tok("vV")]
            for dc in range(NDC):
                TEN(lambda e, t=tt, d=dc, q=pb:
                    e.matmul(q[:, 0:16],
                             slabX[:, d * S + t * 128:d * S + (t + 1) * 128],
                             atT[:, d * 16:(d + 1) * 16],
                             start=(d == 0), stop=(d == NDC - 1)),
                    ([t_xnT, t_atT] + pbw) if dc == 0 else (),
                    inc=(1 if dc == NDC - 1 else 0))
            t_mm = Sch.tok("tM")
            wlex = [prev_lexp] if prev_lexp else []
            t_lsb = VEC(lambda e, q=pb: e.tensor_add(Lsb[:, :], q[:, 0:16],
                                                     cbc[:, :]),
                        [t_mm, t_cbc] + wlex)["vV"]
            pbl_free[tt % 2] = t_lsb
            vgap()
            t_max = VEC(lambda e: e.max(t8[:, :], Lsb[:, :]), [t_lsb])["vV"]
            vgap()
            t_t8 = VEC(lambda e: e.tensor_scalar_mul(nv1[:, :], t8[:, 0:1],
                                                     -1.0), [t_max])["vV"]
            t_exp = SCA(lambda e: e.activation(Ue[:, :], Lsb[:, :], AF.Exp,
                                               bias=nv1[:, 0:1]), [t_t8])["sS"]
            prev_lexp = t_exp
            t_gm = VEC(lambda e: e.tensor_scalar(gemask[:, :], Lsb[:, :],
                                                 t8[:, 3:4], None,
                                                 ALU.is_ge), [t_exp, t_max])["vV"]
            vgap()
            t_ue = VEC(lambda e: e.tensor_mul(Ue[:, :], Ue[:, :], gemask[:, :]),
                       [t_gm])["vV"]
            vgap()
            t_rs = VEC(lambda e: e.reduce_sum(rsum[:, :], Ue[:, :], axis=AX.X),
                       [t_ue])["vV"]
            vgap()
            t_rr2 = VEC(lambda e: e.reciprocal(rrec[:, :], rsum[:, :]),
                        [t_rs])["vV"]
            vgap()
            t_uem = VEC(lambda e: e.tensor_scalar_mul(Ue[:, :], Ue[:, :],
                                                      rrec[:, 0:1]), [t_rr2])["vV"]
            vgap()
            wmb = [prev_mtmm] if prev_mtmm else []
            t_mb = VEC(lambda e: e.tensor_copy(mbf[:, :], Ue[:, :]),
                       [t_uem] + wmb)["vV"]
            TEN(lambda e: e.matmul(pC0[0:16, :], mbf[:, :], id128b[:, :],
                                   start=True, stop=True),
                [t_mb, t_id128b, t_mtprev])
            prev_mtmm = Sch.tok("tM")
            t_mtprev = VEC(lambda e, t=tt: e.tensor_copy(mT[:, t * 128:(t + 1) * 128],
                                                         pC0[0:16, :]),
                           [prev_mtmm])["vV"]
        X["t_mT"] = t_mtprev
        X["t_mTh"] = DMA("sync", mTh_d[:, :], mT[:, :], [t_mtprev])

        Sch.barrier(exclude=("dG",))
        Sch.flush(nc)
        pc2.close()

    with ExitStack() as pmn:
        pHa = ps("pHa", [128, 512], pmn)
        pHb = ps("pHb", [128, 512], pmn)
        pFa = ps("pFa", [128, 1024], pmn)
        pFb = ps("pFb", [128, 1024], pmn)
        pSa = ps("pSa", [128, 512], pmn)
        pSb = ps("pSb", [128, 512], pmn)
        pF = [pFa, pFb]
        pFn = ["pFa", "pFb"]
        pHl = [pHa, pHb]
        pHn = ["pHa", "pHb"]
        pSl = [pSa, pSb]
        pSn = ["pSa", "pSb"]

        xnTt = sb("xnTt", [128, 8 * TKT], BF16, pmn, is_transpose_dest=True)
        w1s = sb("w1s", [128, 8192], BF16, pmn, is_transpose_dest=True)
        w2s = sb("w2s", [128, 8192], BF16, pmn, is_transpose_dest=True)
        Hms = sb("Hms", [128, 8192], BF16, pmn)
        slab = sb("slab", [128, 8192], F32, pmn)
        mbc = sb("mbc", [128, TKT], BF16, pmn)
        gts = [sb("gts0", [128, TKT], BF16, pmn)]
        kst = sb("kst", [128, S], BF16, pmn)
        qst = sb("qst", [128, S], BF16, pmn)
        vhb = sb("vh0", [128, 32 * 65], BF16, pmn)
        Tsb = sb("Tsb", [128, WTS], BF16, pmn)
        S1b = [sb("S1a", [128, S], F32, pmn), sb("S1c", [128, S], F32, pmn)]
        Ub = sb("Ub", [128, S], BF16, pmn)
        UTb = [sb("UT0", [128, S], BF16, pmn, is_transpose_dest=True),
               sb("UT1", [128, S], BF16, pmn, is_transpose_dest=True)]
        pvE = [sb("pvE0", [128, 128], BF16, pmn)]
        sTt = [sb("sT0", [128, 128], BF16, pmn, is_transpose_dest=True),
               sb("sT1", [128, 128], BF16, pmn, is_transpose_dest=True)]
        sO2 = sb("sO0", [128, 64], BF16, pmn)
        mrow = sb("mrow", [1, TKT], BF16, pmn)
        mxc = sb("mxc", [128, 1], F32, pmn)
        nmc = sb("nmc", [128, 1], F32, pmn)
        rcf = sb("rcf", [128, 1], F32, pmn)
        rcr = sb("rcr", [128, 1], F32, pmn)

        VEC(lambda e: e.memset(vhb[:, :], 1.0))
        t_init = VEC(lambda e: e.memset(pvE[0][:, :], 0.0))["vV"]

        free = {k: None for k in
                ("xnTt", "w1s", "w2s", "slab", "mbc", "mrow", "kst", "qst",
                 "Tsb", "Ub", "pHa", "pHb", "pFa", "pFb", "pSa", "pSb")}
        freeL = {"vh": [t_init], "S1": [None, None], "UT": [None, None],
                 "pvE": [None, None], "sT": [None, None], "gts": [None, None],
                 "sO": [None, None]}

        def wf(name):
            t = free[name]
            return [t] if t else []

        fsw = [0]

        for T in range(NTOUT if stages == 'all' else 0):
            t_xnT = None
            for dc in range(NDC):
                t_xnT = DMAT(xnTt[:, dc * TKT:(dc + 1) * TKT],
                             xnh_d[T * TKT:(T + 1) * TKT, dc * 128:(dc + 1) * 128],
                             wf("xnTt"))
            fill_xnT = t_xnT
            for e_i in range(E):
                t_mr = DMA("scalar", mrow[:, :],
                           mTh_d[e_i:e_i + 1, T * TKT:(T + 1) * TKT],
                           [X["t_mTh"]] + wf("mrow"))
                for t2 in range(2):
                    TEN(lambda e, t=t2:
                        e.matmul(pHa[:, :], onesb[:, :],
                                 mrow[0:1, t * 512:(t + 1) * 512],
                                 start=True, stop=True),
                        [t_mr] + wf("pHa"))
                    free["pHa"] = VEC(
                        lambda e, t=t2: e.tensor_copy(mbc[:, t * 512:(t + 1) * 512],
                                                      pHa[:, :]),
                        [Sch.tok("tM")])["vV"]
                free["mrow"] = Sch.tok("tM")
                fill_mbc = free["pHa"]
                for q in range(4):
                    wex = [X["repack_done"][e_i]] if T == 0 else []
                    t_w1 = DMA("sync",
                               w1s[:, :].rearrange("p (b f) -> p b f", b=8),
                               bass.AP(w1tb_d.tensor,
                                       e_i * 8 * 128 * F + q * 1024,
                                       [[F, 128], [128 * F, 8], [1, 1024]]),
                               wf("w1s") + wex)
                    t_w2 = DMA("sync",
                               w2s[:, :].rearrange("p (b f) -> p b f", b=8),
                               bass.AP(w2tb_d.tensor,
                                       e_i * 32 * 128 * D + q * 8 * 128 * D,
                                       [[D, 128], [128 * D, 8], [1, 1024]]),
                               wf("w2s") + wex)
                    for t2 in range(2):
                        for ft in range(8):
                            hi = (t2 * 8 + ft) % 2
                            for dc in range(NDC):
                                TEN(lambda e, d=dc, f=ft, t=t2, ph=pHl[hi]:
                                    e.matmul(ph[:, :],
                                             w1s[:, d * 1024 + f * 128:
                                                 d * 1024 + (f + 1) * 128],
                                             xnTt[:, d * TKT + t * 512:
                                                  d * TKT + (t + 1) * 512],
                                             start=(d == 0), stop=(d == NDC - 1)),
                                    ([t_w1, fill_xnT] + wf(pHn[hi]))
                                    if dc == 0 else (),
                                    inc=(1 if dc == NDC - 1 else 0))
                            t_mm = Sch.tok("tM")
                            SCA(lambda e, f=ft, t=t2, ph=pHl[hi], ei=e_i, qq=q:
                                e.activation(Hms[:, f * 1024 + t * 512:
                                                 f * 1024 + (t + 1) * 512],
                                             ph[:, :], AF.Gelu,
                                             bias=b1T[:, (qq * 8 + f) * 16 + ei:
                                                      (qq * 8 + f) * 16 + ei + 1]),
                                [t_mm, X["t_b1T"]])
                            free[pHn[hi]] = Sch.tok("sS")
                    free["w1s"] = Sch.tok("tM")
                    t_gelu = Sch.tok("sS")
                    for ft in range(8):
                        VEC(lambda e, f=ft: e.tensor_mul(
                            Hms[:, f * 1024:(f + 1) * 1024],
                            Hms[:, f * 1024:(f + 1) * 1024], mbc[:, :]),
                            [t_gelu, fill_mbc])
                    t_mask = Sch.tok("vV")
                    free["mbc"] = t_mask
                    for t2 in range(2):
                        for dtp in range(4):
                            fi = fsw[0] % 2
                            fsw[0] += 1
                            ph = pF[fi]
                            w0 = [t_mask, t_w2] + wf(pFn[fi])
                            nmm = 0
                            for ddt in range(2):
                                for fc in range(8):
                                    TEN(lambda e, dd=ddt, f=fc, t=t2, dp=dtp, p=ph:
                                        e.matmul(p[:, dd * 512:(dd + 1) * 512],
                                                 w2s[:, f * 1024 + (dp * 2 + dd) * 128:
                                                     f * 1024 + (dp * 2 + dd) * 128 + 128],
                                                 Hms[:, f * 1024 + t * 512:
                                                     f * 1024 + (t + 1) * 512],
                                                 start=(f == 0), stop=(f == 7)),
                                        w0 if nmm == 0 else (),
                                        inc=(1 if (ddt == 1 and fc == 7) else 0))
                                    nmm += 1
                            t_m2 = Sch.tok("tM")
                            sl_ap = slab[:, :].rearrange("p (dt t) -> p dt t", t=TKT)[
                                :, dtp * 2:dtp * 2 + 2, t2 * 512:(t2 + 1) * 512]
                            pf_ap = ph[:, :].rearrange("p (dd t) -> p dd t", t=512)
                            if e_i == 0 and q == 0:
                                free[pFn[fi]] = VEC(
                                    lambda e, o=sl_ap, i=pf_ap: e.tensor_copy(o, i),
                                    [t_m2] + wf("slab"))["vV"]
                            else:
                                free[pFn[fi]] = VEC(
                                    lambda e, o=sl_ap, i=pf_ap: e.tensor_add(o, o, i),
                                    [t_m2])["vV"]
                    free["w2s"] = Sch.tok("tM")
            for dt in range(8):
                for t2 in range(2):
                    fi = fsw[0] % 2
                    fsw[0] += 1
                    ph = pF[fi]
                    TEN(lambda e, d=dt, t=t2, TT=T, p=ph:
                        e.matmul(p[:, 0:512], b2b[:, d * 128:(d + 1) * 128],
                                 mT[:, TT * TKT + t * 512:TT * TKT + (t + 1) * 512],
                                 start=True, stop=True),
                        [X["t_mT"], X["t_b2"]] + wf(pFn[fi]))
                    t_b2m = Sch.tok("tM")
                    free[pFn[fi]] = VEC(
                        lambda e, d=dt, t=t2, p=ph: e.tensor_add(
                            slab[:, d * TKT + t * 512:d * TKT + (t + 1) * 512],
                            slab[:, d * TKT + t * 512:d * TKT + (t + 1) * 512],
                            p[:, 0:512]),
                        [t_b2m])["vV"]
            t_slab = Sch.tok("vV")
            for dt in range(8):
                gbuf = gts[0]
                w = [t_slab] + ([freeL["gts"][0]] if freeL["gts"][0] else [])
                t_si = SCA(lambda e, d=dt, b=gbuf:
                           e.activation(b[:, :], slab[:, d * TKT:(d + 1) * TKT],
                                        AF.Silu), w)["sS"]
                freeL["gts"][0] = DMA(
                    "sync", gTh_d[dt * 128:(dt + 1) * 128, T * TKT:(T + 1) * TKT],
                    gbuf[:, :], [t_si])
            free["slab"] = Sch.tok("sS")
            free["xnTt"] = Sch.tok("tM")

            for hh in range(4):
                h = 4 * T + hh
                t_k = DMA("scalar", kst[:, :], stkK_d[h, :, :], wf("kst"))
                t_q = DMA("scalar", qst[:, :], stkQ_d[h, :, :], wf("qst"))
                vh = vhb
                t_v = DMA("scalar",
                          vh[:, :].rearrange("p (tt j) -> p tt j", j=65)[:, :, 0:64],
                          bass.AP(vtm_d.tensor, h * 64,
                                  [[D, 128], [128 * D, 32], [1, 64]]),
                          [freeL["vh"][0], X["t_vtm"]])
                t_T = DMA("scalar", Tsb[:, :],
                          bass.AP(texth_d.tensor, h * 8192, [[1, 128], [1, WTS]]),
                          wf("Tsb") + [X["t_text"]])
                prev_aexp = free.get("prev_aexp")
                for qt in range(NTT):
                    Q0 = qt * 128
                    s1 = S1b[qt % 2]
                    s1w = [freeL["S1"][qt % 2]] if freeL["S1"][qt % 2] else []
                    for kc in range(8):
                        si = kc % 2
                        TEN(lambda e, k=kc, q0=Q0, p=pSl[si]:
                            e.matmul(p[:, :], qst[:, q0:q0 + 128],
                                     kst[:, k * 512:(k + 1) * 512],
                                     start=True, stop=True),
                            [t_k, t_q] + wf(pSn[si]))
                        t_sc = Sch.tok("tM")
                        a0 = Q0 + 3584 - 512 * kc
                        free[pSn[si]] = VEC(
                            lambda e, k=kc, aa=a0, b=s1, p=pSl[si]:
                            e.tensor_add(b[:, k * 512:(k + 1) * 512], p[:, :],
                                         Tsb[:, aa:aa + 512][:, ::-1]),
                            [t_sc, t_T] + (s1w if kc == 0 else []))["vV"]
                    t_park = Sch.tok("vV")
                    free["Tsb"] = t_park
                    wax = [prev_aexp] if prev_aexp else []
                    t_mx = VEC(lambda e, b=s1: e.reduce_max(mxc[:, :], b[:, :],
                                                            axis=AX.X), wax)["vV"]
                    vgap()
                    t_nm = VEC(lambda e: e.tensor_scalar_mul(nmc[:, :], mxc[:, :],
                                                             -CSC), [t_mx])["vV"]
                    t_exp = SCA(lambda e, b=s1: e.activation(Ub[:, :], b[:, :],
                                                             AF.Exp,
                                                             bias=nmc[:, 0:1],
                                                             scale=CSC),
                                [t_nm] + wf("Ub"))["sS"]
                    freeL["S1"][qt % 2] = t_exp
                    prev_aexp = t_exp
                    free["prev_aexp"] = t_exp
                    ut = UTb[qt % 2]
                    utw = [freeL["UT"][qt % 2]] if freeL["UT"][qt % 2] else []
                    t_tr = DMAT(ut[:, :].rearrange("p (c f) -> p c f", c=32),
                                Ub[:, :].rearrange("p (c f) -> p c f", c=32),
                                [t_exp] + utw)
                    free["Ub"] = t_tr
                    for kc2 in range(32):
                        TEN(lambda e, k=kc2, u=ut, v=vh:
                            e.matmul(pFa[0:65, 0:128], v[:, k * 65:(k + 1) * 65],
                                     u[:, k * 128:(k + 1) * 128],
                                     start=(k == 0), stop=(k == 31)),
                            ([t_tr, t_v] + wf("pFa")) if kc2 == 0 else (),
                            inc=(1 if kc2 == 31 else 0))
                    t_pv = Sch.tok("tM")
                    freeL["UT"][qt % 2] = t_pv
                    pv = pvE[0]
                    pew = [freeL["pvE"][0]] if freeL["pvE"][0] else []
                    t_ev = VEC(lambda e, b=pv: e.tensor_copy(b[0:65, :],
                                                             pFa[0:65, 0:128]),
                               [t_pv] + pew)["vV"]
                    free["pFa"] = t_ev
                    st = sTt[qt % 2]
                    stw = [freeL["sT"][qt % 2]] if freeL["sT"][qt % 2] else []
                    t_str = DMAT(st[:, :], pv[:, :], [t_ev] + stw)
                    freeL["pvE"][0] = t_str
                    so = sO2
                    sow = [freeL["sO"][0]] if freeL["sO"][0] else []
                    t_rcf = VEC(lambda e, b=st: e.tensor_copy(rcf[:, :],
                                                              b[:, 64:65]),
                                [t_str])["vV"]
                    vgap()
                    t_rc2 = VEC(lambda e: e.reciprocal(rcr[:, :], rcf[:, :]),
                                [t_rcf])["vV"]
                    vgap()
                    t_div = VEC(lambda e, b=st, o=so: e.tensor_scalar_mul(
                        o[:, :], b[:, 0:64], rcr[:, 0:1]), sow + [t_rc2])["vV"]
                    freeL["sT"][qt % 2] = t_div
                    freeL["sO"][0] = DMA(
                        "scalar", sh_d[Q0:Q0 + 128, h * 64:(h + 1) * 64],
                        so[:, :], [t_div])
                free["kst"] = Sch.tok("tM")
                free["qst"] = Sch.tok("tM")
                freeL["vh"][0] = Sch.tok("tM")

        Sch.barrier()
        Sch.flush(nc)

    with ExitStack() as pcm:
        xt2 = [sb("cx0", [128, D], F32, pcm), sb("cx1", [128, D], F32, pcm)]
        st2 = [sb("cs0", [128, D], BF16, pcm), sb("cs1", [128, D], BF16, pcm)]
        gt2 = [sb("cg0", [128, D], BF16, pcm, is_transpose_dest=True),
               sb("cg1", [128, D], BF16, pcm, is_transpose_dest=True)]
        tm2 = [sb("ct0", [128, D], F32, pcm), sb("ct1", [128, D], F32, pcm)]
        frees = [None, None]
        for tt in range(NTT if stages == 'all' else 0):
            i = tt % 2
            w = [frees[i]] if frees[i] else []
            t_x = DMA("scalar", xt2[i][:, :], x_d[tt * 128:(tt + 1) * 128, :], w)
            t_s = DMA("scalar", st2[i][:, :], sh_d[tt * 128:(tt + 1) * 128, :], w)
            t_g = DMAT(gt2[i][:, :], gTh_d[0:D, tt * 128:(tt + 1) * 128], w)
            VEC(lambda e, ii=i: e.tensor_mul(tm2[ii][:, :], st2[ii][:, :],
                                             gt2[ii][:, :]), [t_x, t_s, t_g])
            t_o = VEC(lambda e, ii=i: e.tensor_add(tm2[ii][:, :], tm2[ii][:, :],
                                                   xt2[ii][:, :]))["vV"]
            frees[i] = DMA("sync", y_d[tt * 128:(tt + 1) * 128, :], tm2[i][:, :],
                           [t_o])

        if debug:
            t_all = [Sch.tok(s) for s in ("dS", "dA", "dG", "tM", "vV", "sS")]
            for src, dst in [(xnh_d, dbg["dbg_xn"]), (xn2h_d, dbg["dbg_xn2"]),
                             (peh_d, dbg["dbg_pe"]), (vtm_d, dbg["dbg_v"]),
                             (texth_d, dbg["dbg_text"]), (sh_d, dbg["dbg_s"]),
                             (gTh_d, dbg["dbg_g"])]:
                DMA("gpsimd", dst[:, :], src[:, :], t_all)
            DMA("gpsimd", dbg["dbg_stkK"][:, :, :], stkK_d[:, :, :], t_all)
            DMA("gpsimd", dbg["dbg_stkQ"][:, :, :], stkQ_d[:, :, :], t_all)
            DMA("gpsimd", dbg["dbg_mT"][:, :], mT[:, :], t_all)

        final_waits = [Sch.tok(s) for s in ("dS", "dA", "dG")]
        Sch.do("sync", lambda e: e.nop(), final_waits, ())
        Sch.flush(nc)

    ctx.close()
    return nc


_NC_CACHE = {}


def kernel(L0_attn_x, L0_attn_gamma, L0_attn_beta, L0_attn_w_pos, L0_attn_w_tok,
           L0_attn_bias_table, L0_ffn_keys, L0_ffn_w1, L0_ffn_b1, L0_ffn_w2,
           L0_ffn_b2, L0_ffn_rw, L0_ffn_rb):
    B = L0_attn_x.shape[0]
    key = ("nc", DEBUG, STAGES)
    if key not in _NC_CACHE:
        _NC_CACHE[key] = build_nc(debug=DEBUG, stages=STAGES)
    nc = _NC_CACHE[key]

    f32 = np.float32
    common = dict(
        gamma=np.ascontiguousarray(np.asarray(L0_attn_gamma, f32).reshape(1, D)),
        beta=np.ascontiguousarray(np.asarray(L0_attn_beta, f32).reshape(1, D)),
        w_pos=np.ascontiguousarray(np.asarray(L0_attn_w_pos, f32)),
        w_tok=np.ascontiguousarray(np.asarray(L0_attn_w_tok, f32)),
        bias_table=np.ascontiguousarray(np.asarray(L0_attn_bias_table, f32)),
        keys=np.ascontiguousarray(np.asarray(L0_ffn_keys, f32)),
        rw=np.ascontiguousarray(np.asarray(L0_ffn_rw, f32)),
        rb=np.ascontiguousarray(np.asarray(L0_ffn_rb, f32).reshape(E, 1)),
        w1=np.ascontiguousarray(np.asarray(L0_ffn_w1, f32)),
        b1=np.ascontiguousarray(np.asarray(L0_ffn_b1, f32)),
        w2=np.ascontiguousarray(np.asarray(L0_ffn_w2, f32)),
        b2=np.ascontiguousarray(np.asarray(L0_ffn_b2, f32)),
        id16f=np.eye(16, dtype=f32),
        id16b=np.eye(16, dtype=f32).astype(ml_dtypes.bfloat16),
        id128f=np.eye(128, dtype=f32),
        id128b=np.eye(128, dtype=f32).astype(ml_dtypes.bfloat16),
    )
    xs = np.ascontiguousarray(np.asarray(L0_attn_x, f32))
    in_maps = [dict(common, x=np.ascontiguousarray(xs[b])) for b in range(B)]
    res = run_bass_kernel_spmd(nc, in_maps, core_ids=list(range(B)))
    out = np.stack([np.asarray(r["y"], f32) for r in res.results], axis=0)
    if DEBUG:
        kernel._dbg = res.results
    return out


# version 10 (final)
# speedup vs baseline: 1.1626x; 1.1626x over previous
import math
from contextlib import ExitStack

import numpy as np
import ml_dtypes
import concourse.bass as bass
import concourse.mybir as mybir
from concourse.bass_utils import run_bass_kernel_spmd

F32 = mybir.dt.float32
BF16 = mybir.dt.bfloat16
AF = mybir.ActivationFunctionType
ALU = mybir.AluOpType
AX = mybir.AxisListType

S, D, H, DH, E, F = 4096, 1024, 16, 64, 16, 4096
NTT = S // 128
NDC = D // 128
NTOUT = 4
TKT = S // NTOUT
CSC = float(math.sqrt(2 * DH))
WTS = 8064
MAGIC = 12582912.0
INV2PI = 1.0 / (2.0 * math.pi)
TWOPI = 2.0 * math.pi

DEBUG = False
STAGES = "all"

ENGS = ("tensor", "vector", "scalar", "sync", "gpsimd")


class Sched:

    def __init__(self, semh):
        self.ops = {e: [] for e in ENGS}
        self.cnt = {s: 0 for s in semh}
        self.semh = semh
        self.last_wait = {e: {} for e in ENGS}

    def do(self, eng, emit, wait=(), inc=()):
        inc = tuple(inc)
        for s, k in inc:
            self.cnt[s] += k
        self.ops[eng].append((emit, tuple(w for w in wait if w is not None), inc))
        return {s: (s, self.cnt[s]) for s, _ in inc}

    def tok(self, sem):
        return (sem, self.cnt[sem])

    def barrier(self, exclude=()):
        waits = [self.tok(s) for s in self.cnt if s not in exclude]
        for eng in ENGS:
            self.do(eng, lambda e: e.nop(), waits, ())

    def flush(self, nc):
        def mk_body(eng):
            ops = self.ops[eng]
            lw = self.last_wait[eng]

            def body(proxy):
                for emit, waits, incs in ops:
                    for s, v in waits:
                        if lw.get(s, -1) < v:
                            proxy.wait_ge(self.semh[s], v)
                            lw[s] = v
                    ins = emit(proxy)
                    for s, k in incs:
                        ins.then_inc(self.semh[s], k)
            return body

        with nc.Block() as block:
            for eng in ENGS:
                if self.ops[eng]:
                    getattr(block, eng)(mk_body(eng))
        self.ops = {e: [] for e in ENGS}


def build_nc(debug=False, stages='all'):
    nc = bass.Bass("TRN2", target_bir_lowering=False)

    x_d = nc.dram_tensor("x", [S, D], F32, kind="ExternalInput").ap()
    gamma_d = nc.dram_tensor("gamma", [1, D], F32, kind="ExternalInput").ap()
    beta_d = nc.dram_tensor("beta", [1, D], F32, kind="ExternalInput").ap()
    wpos_d = nc.dram_tensor("w_pos", [2 * D, D], F32, kind="ExternalInput").ap()
    wtok_d = nc.dram_tensor("w_tok", [3 * D, D], F32, kind="ExternalInput").ap()
    btab_d = nc.dram_tensor("bias_table", [2048, H], F32, kind="ExternalInput").ap()
    keys_d = nc.dram_tensor("keys", [E, D], F32, kind="ExternalInput").ap()
    rw_d = nc.dram_tensor("rw", [E, D], F32, kind="ExternalInput").ap()
    rb_d = nc.dram_tensor("rb", [E, 1], F32, kind="ExternalInput").ap()
    w1_d = nc.dram_tensor("w1", [E, F, D], F32, kind="ExternalInput").ap()
    b1_d = nc.dram_tensor("b1", [E, F], F32, kind="ExternalInput").ap()
    w2_d = nc.dram_tensor("w2", [E, D, F], F32, kind="ExternalInput").ap()
    b2_d = nc.dram_tensor("b2", [E, D], F32, kind="ExternalInput").ap()
    id16f_d = nc.dram_tensor("id16f", [16, 16], F32, kind="ExternalInput").ap()
    id16b_d = nc.dram_tensor("id16b", [16, 16], BF16, kind="ExternalInput").ap()
    id128f_d = nc.dram_tensor("id128f", [128, 128], F32, kind="ExternalInput").ap()
    id128b_d = nc.dram_tensor("id128b", [128, 128], BF16, kind="ExternalInput").ap()
    y_d = nc.dram_tensor("y", [S, D], F32, kind="ExternalOutput").ap()

    w1b_d = nc.dram_tensor("w1b", [E, F, D], BF16).ap()
    w2b_d = nc.dram_tensor("w2b", [E, D, F], BF16).ap()
    w1tb_d = nc.dram_tensor("w1tb", [E, 8, 128, F], BF16).ap()
    w2tb_d = nc.dram_tensor("w2tb", [E, 32, 128, D], BF16).ap()
    wtokb_d = nc.dram_tensor("wtokb", [3 * D, D], BF16).ap()
    wposb_d = nc.dram_tensor("wposb", [2 * D, D], BF16).ap()
    xnh_d = nc.dram_tensor("xnh", [S, D], BF16).ap()
    xn2h_d = nc.dram_tensor("xn2h", [S, D], BF16).ap()
    peh_d = nc.dram_tensor("peh", [S, D], BF16).ap()
    stkK_d = nc.dram_tensor("stkK", [H, 128, S], BF16).ap()
    stkQ_d = nc.dram_tensor("stkQ", [H, 128, S], BF16).ap()
    vTh_d = nc.dram_tensor("vTh", [D, S], BF16).ap()
    vtm_d = nc.dram_tensor("vtm", [S, D], BF16).ap()
    texth_d = nc.dram_tensor("texth", [H, 8192], BF16).ap()
    sh_d = nc.dram_tensor("sh", [S, D], BF16).ap()
    mTh_d = nc.dram_tensor("mTh", [16, S], BF16).ap()
    gTh_d = nc.dram_tensor("gTh", [D, S], BF16).ap()

    dbg = {}
    if debug:
        for name, shp, dt in [
            ("dbg_xn", [S, D], BF16), ("dbg_xn2", [S, D], BF16),
            ("dbg_pe", [S, D], BF16), ("dbg_stkK", [H, 128, S], BF16),
            ("dbg_stkQ", [H, 128, S], BF16), ("dbg_v", [S, D], BF16),
            ("dbg_text", [H, 8192], BF16), ("dbg_s", [S, D], BF16),
            ("dbg_g", [D, S], BF16), ("dbg_mT", [16, S], BF16),
        ]:
            dbg[name] = nc.dram_tensor(name, shp, dt, kind="ExternalOutput").ap()

    ctx = ExitStack()
    semh = {}
    for s in ("dS", "dA", "dG", "tM", "vV", "sS", "gG"):
        semh[s] = ctx.enter_context(nc.semaphore(s))
    Sch = Sched(semh)

    def DMA(q, dst, src, wait=()):
        sem = {"sync": "dS", "scalar": "dA", "gpsimd": "dG"}[q]
        return Sch.do(q, lambda e, a=dst, b=src: e.dma_start(a, b), wait,
                      ((sem, 16),))[sem]

    def DMAT(dst, src, wait=()):
        return Sch.do("sync", lambda e, a=dst, b=src: e.dma_start_transpose(a, b),
                      wait, (("dS", 16),))["dS"]

    def TEN(emit, wait=(), inc=1):
        return Sch.do("tensor", emit, wait, ((("tM", inc),) if inc else ()))

    def VEC(emit, wait=(), inc=1):
        return Sch.do("vector", emit, wait, ((("vV", inc),) if inc else ()))

    def SCA(emit, wait=(), inc=1):
        return Sch.do("scalar", emit, wait, ((("sS", inc),) if inc else ()))

    def sb(name, shape, dt, stack=None, **kw):
        return (stack or ctx).enter_context(nc.sbuf_tensor(name, shape, dt, **kw))

    def vgap(n=2):
        for _ in range(n):
            Sch.do("vector", lambda e: e.tensor_scalar_mul(junk[:, 0:1],
                                                           junk[:, 0:1], 1.0),
                   (), ())

    def vlate(emit, wait=()):
        Sch.do("vector", emit, wait, ())
        Sch.do("vector", lambda e: e.tensor_scalar_mul(junk[:, 0:1],
                                                       junk[:, 0:1], 1.0), (), ())
        return Sch.do("vector", lambda e: e.tensor_scalar_mul(junk[:, 1:2],
                                                              junk[:, 1:2], 1.0),
                      (), (("vV", 1),))["vV"]

    def ps(name, shape, stack=None):
        return (stack or ctx).enter_context(nc.psum_tensor(name, shape, F32))

    b1T = sb("b1T", [128, 32 * 16], F32)
    b2b = sb("b2b", [16, D], BF16)
    mT = sb("mT", [16, S], BF16)
    onesb = sb("onesb", [1, 128], BF16)
    onesf = sb("onesf", [1, 128], F32)
    epsc = sb("epsc", [128, 1], F32)
    junk = sb("junk", [128, 2], F32)

    X = {}

    with ExitStack() as p0:
        pA0 = ps("pA0", [128, 1024], p0)
        pB0a = ps("pB0a", [128, 512], p0)
        pB0b = ps("pB0b", [128, 512], p0)
        pC0 = ps("pC0", [128, 128], p0)
        pD0 = ps("pD0", [128, 512], p0)

        gbc = sb("gbc", [128, D], F32, p0)
        bbc = sb("bbc", [128, D], F32, p0)
        cw = [sb("cw0", [128, 8192], BF16, p0), sb("cw1", [128, 8192], BF16, p0)]
        cbc = sb("cbc", [128, 16], F32, p0)
        atT = sb("atT", [128, 128], BF16, p0)
        id128b = sb("s_id128b", [128, 128], BF16, p0)
        pa = ExitStack()
        id16f = sb("s_id16f", [16, 16], F32, pa)
        id16b = sb("s_id16b", [16, 16], BF16, pa)
        id128f = sb("s_id128f", [128, 128], F32, pa)
        grow = sb("grow", [1, D], F32, pa)
        brow = sb("brow", [1, D], F32, pa)

        t_id16f = DMA("scalar", id16f[:, :], id16f_d[:, :])
        t_id16b = DMA("scalar", id16b[:, :], id16b_d[:, :])
        t_id128f = DMA("scalar", id128f[:, :], id128f_d[:, :])
        t_id128b = DMA("scalar", id128b[:, :], id128b_d[:, :])
        t_grow = DMA("scalar", grow[:, :], gamma_d[:, :])
        t_brow = DMA("scalar", brow[:, :], beta_d[:, :])
        VEC(lambda e: e.memset(onesb[:, :], 1.0))
        VEC(lambda e: e.memset(junk[:, :], 1.0))
        VEC(lambda e: e.memset(epsc[:, :], 1e-5))
        t_ones = VEC(lambda e: e.memset(onesf[:, :], 1.0))["vV"]

        for half in range(2):
            TEN(lambda e, h=half: e.matmul(pA0[:, h * 512:(h + 1) * 512], onesf[:, :],
                                           grow[0:1, h * 512:(h + 1) * 512],
                                           start=True, stop=True),
                [t_grow, t_ones], inc=(1 if half == 1 else 0))
        t_gmm2 = Sch.tok("tM")
        vgap(3)
        t_gbc = VEC(lambda e: e.tensor_copy(gbc[:, :], pA0[:, :]),
                    [t_gmm2])["vV"]
        for half in range(2):
            TEN(lambda e, h=half: e.matmul(pA0[:, h * 512:(h + 1) * 512], onesf[:, :],
                                           brow[0:1, h * 512:(h + 1) * 512],
                                           start=True, stop=True),
                [t_brow, ("vV", t_gbc[1])], inc=(1 if half == 1 else 0))
        t_bmm2 = Sch.tok("tM")
        vgap(3)
        t_bbc = VEC(lambda e: e.tensor_copy(bbc[:, :], pA0[:, :]),
                    [t_bmm2])["vV"]

        cw_free = [None, None]
        cast_done = {}
        cast_jobs = []
        for c in range(3):
            cast_jobs.append(("wtok", wtok_d, wtokb_d, c * 8 * 128 * D, 8, D))
        for c in range(2):
            cast_jobs.append(("wpos", wpos_d, wposb_d, c * 8 * 128 * D, 8, D))
        for e_i in range(E):
            for c in range(4):
                cast_jobs.append((f"w1_{e_i}", w1_d, w1b_d,
                                  e_i * F * D + c * 8 * 128 * D, 8, D))
            for c in range(4):
                cast_jobs.append((f"w2_{e_i}", w2_d, w2b_d,
                                  e_i * D * F + c * 2 * 128 * F, 2, F))
        for j, (tag, src_d, dst_d, off, nblk, inner) in enumerate(cast_jobs):
            buf = cw[j % 2]
            src_ap = bass.AP(src_d.tensor, off,
                             [[inner, 128], [128 * inner, nblk], [1, inner]])
            dst_ap = bass.AP(dst_d.tensor, off,
                             [[inner, 128], [128 * inner, nblk], [1, inner]])
            bview = buf[:, 0:nblk * inner].rearrange("p (b i) -> p b i", b=nblk)
            w = [cw_free[j % 2]] if cw_free[j % 2] else []
            t_in = DMA("gpsimd", bview, src_ap, w)
            t_out = DMA("gpsimd", dst_ap, bview, [t_in])
            cw_free[j % 2] = t_out
            if tag == "wtok":
                t_wtok_cast = t_out
            elif tag == "wpos":
                t_wpos_cast = t_out
            elif tag.startswith("w2_"):
                cast_done[int(tag.split("_")[1])] = t_out
        X["cast_done"] = cast_done

        btn = sb("btn", [128, 256], F32, pa)
        btT = sb("btT", [16, 2048], F32, pa)
        arr = sb("arr", [16, 8192], F32, pa)
        arrb = sb("arrb", [16, 8192], BF16, pa)
        t_btn = None
        for blk in range(16):
            t_btn = DMA("scalar", btn[:, blk * 16:(blk + 1) * 16],
                        btab_d[blk * 128:(blk + 1) * 128, :])
        t_cp = None
        for blk in range(16):
            w = [t_btn, t_id128f] + ([t_cp] if t_cp else [])
            TEN(lambda e, b=blk: e.matmul(pC0[0:16, :], btn[:, b * 16:(b + 1) * 16],
                                          id128f[:, :], start=True, stop=True), w)
            t_btm = Sch.tok("tM")
            vgap(3)
            t_cp = VEC(lambda e, b=blk: e.tensor_copy(btT[:, b * 128:(b + 1) * 128],
                                                      pC0[0:16, :]),
                       [t_btm])["vV"]
        VEC(lambda e: e.memset(arr[:, :], 1.0))
        t_btsc = VEC(lambda e: e.tensor_scalar_mul(btT[:, :], btT[:, :],
                                                   1.0 / CSC))["vV"]
        VEC(lambda e: e.tensor_copy(arr[:, 3072:5120], btT[:, ::-1]), [t_btsc])
        VEC(lambda e: e.tensor_copy(arr[:, 5120:7168], btT[:, ::-1]))
        VEC(lambda e: e.tensor_scalar_mul(arr[:, 0:3072], arr[:, 0:3072],
                                          btT[:, 2047:2048]), [t_btsc])
        VEC(lambda e: e.tensor_scalar_mul(arr[:, 7168:8192], arr[:, 7168:8192],
                                          btT[:, 0:1]))
        t_arr = VEC(lambda e: e.tensor_copy(arrb[:, :], arr[:, :]))["vV"]
        X["t_text"] = DMA("scalar", texth_d[:, :], arrb[:, :], [t_arr])

        keysb = sb("keysb", [16, D], F32, pa)
        rwb = sb("rwb", [16, D], F32, pa)
        Af = sb("Af", [16, D], F32, pa)
        Abf = sb("Abf", [16, D], BF16, pa)
        k2 = sb("k2", [16, D], F32, pa)
        kcol = sb("kcol", [16, 1], F32, pa)
        rbcol = sb("rbcol", [16, 1], F32, pa)
        ccol = sb("ccol", [16, 1], F32, pa)
        crow = sb("crow", [1, 16], F32, pa)
        t_keys = DMA("scalar", keysb[:, :], keys_d[:, :])
        t_rw = DMA("scalar", rwb[:, :], rw_d[:, :])
        t_rb = DMA("scalar", rbcol[:, :], rb_d[:, :])
        t_af1 = VEC(lambda e: e.tensor_scalar_mul(Af[:, :], keysb[:, :], 2.0),
                    [t_keys])["vV"]
        vgap()
        t_af2 = VEC(lambda e: e.tensor_add(Af[:, :], Af[:, :], rwb[:, :]),
                    [t_rw, t_af1])["vV"]
        vgap()
        t_Abf = VEC(lambda e: e.tensor_copy(Abf[:, :], Af[:, :]), [t_af2])["vV"]
        t_k2 = SCA(lambda e: e.activation(k2[:, :], keysb[:, :], AF.Square),
                   [t_keys])["sS"]
        t_kc = VEC(lambda e: e.reduce_sum(kcol[:, :], k2[:, :], axis=AX.X),
                   [t_k2])["vV"]
        vgap()
        t_ccol = VEC(lambda e: e.tensor_sub(ccol[:, :], rbcol[:, :], kcol[:, :]),
                     [t_rb, t_kc])["vV"]
        TEN(lambda e: e.matmul(pC0[0:1, 0:16], ccol[:, :], id16f[:, :],
                               start=True, stop=True),
            [t_ccol, t_id16f, t_cp])
        t_crm = Sch.tok("tM")
        vgap(3)
        t_crow = VEC(lambda e: e.tensor_copy(crow[:, :], pC0[0:1, 0:16]),
                     [t_crm])["vV"]
        TEN(lambda e: e.matmul(pC0[:, 0:16], onesf[:, :], crow[:, :],
                               start=True, stop=True), [t_crow])
        t_cbm = Sch.tok("tM")
        vgap(3)
        t_cbc = VEC(lambda e: e.tensor_copy(cbc[:, :], pC0[:, 0:16]),
                    [t_cbm])["vV"]
        t_at = t_cbc
        for dc in range(NDC):
            TEN(lambda e, d=dc: e.matmul(pC0[:, 0:16], Abf[:, d * 128:(d + 1) * 128],
                                         id16b[:, :], start=True, stop=True),
                [t_Abf, t_id16b, t_at])
            t_atm = Sch.tok("tM")
            vgap(3)
            t_at = VEC(lambda e, d=dc: e.tensor_copy(atT[:, d * 16:(d + 1) * 16],
                                                     pC0[:, 0:16]),
                       [t_atm])["vV"]
        t_atT = t_at

        b1sb = sb("b1sb", [16, F], F32, pa)
        t_b1 = DMA("scalar", b1sb[:, :], b1_d[:, :])
        t_bb = t_atT
        for ft in range(32):
            TEN(lambda e, f=ft: e.matmul(pC0[:, 0:16], b1sb[:, f * 128:(f + 1) * 128],
                                         id16f[:, :], start=True, stop=True),
                [t_b1, t_id16f, t_bb])
            t_b1m = Sch.tok("tM")
            vgap(3)
            t_bb = VEC(lambda e, f=ft: e.tensor_copy(b1T[:, f * 16:(f + 1) * 16],
                                                     pC0[:, 0:16]),
                       [t_b1m])["vV"]
        X["t_b1T"] = t_bb
        X["t_b2"] = DMA("gpsimd", b2b[:, :], b2_d[:, :])

        Sch.barrier(exclude=("dG",))
        Sch.flush(nc)
        pa.close()

        pb = ExitStack()
        iota = sb("iota", [1, S], F32, pb)
        divr = sb("divr", [1, 512], F32, pb)
        Sch.do("gpsimd", lambda e: e.iota(iota[:, :], [[1, S]], channel_multiplier=0,
                                          allow_small_or_imprecise_dtypes=True),
               (), (("gG", 1),))
        t_iota = Sch.tok("gG")
        t_div = SCA(lambda e: e.activation(divr[:, :], iota[0:1, 0:512], AF.Exp,
                                           scale=-math.log(10000.0) / 512.0),
                    [t_iota])["sS"]
        pet = [sb("pet0", [128, D], BF16, pb), sb("pet1", [128, D], BF16, pb)]
        rr = [sb(f"rr{i}", [128, 512], F32, pb) for i in range(3)]
        pet_free = [None, None]
        t_sin_prev = None
        t_r1_prev = None
        for tt in range(NTT):
            buf = pet[tt % 2]
            w = [t_div, t_iota] + ([t_r1_prev] if t_r1_prev else [])
            TEN(lambda e, t=tt: e.matmul(pD0[:, :], iota[0:1, t * 128:(t + 1) * 128],
                                         divr[:, :], start=True, stop=True), w)
            t_ang = Sch.tok("tM")
            war = [t_sin_prev] if tt > 0 else []
            vgap(3)
            t_r1_prev = VEC(lambda e: e.tensor_scalar_mul(rr[0][:, :], pD0[:, :],
                                                          INV2PI), [t_ang] + war)["vV"]
            VEC(lambda e: e.tensor_scalar_add(rr[1][:, :], rr[0][:, :], MAGIC))
            VEC(lambda e: e.tensor_scalar_add(rr[1][:, :], rr[1][:, :], -MAGIC))
            t_r4 = VEC(lambda e: e.tensor_sub(rr[1][:, :], rr[0][:, :],
                                              rr[1][:, :]))["vV"]
            w_buf = [pet_free[tt % 2]] if pet_free[tt % 2] else []
            sin_view = buf[:, :].rearrange("p (i two) -> p two i", two=2)
            SCA(lambda e, a=sin_view: e.activation(a[:, 0:1, :], rr[1][:, :], AF.Sin,
                                                   scale=TWOPI), [t_r4] + w_buf)
            VEC(lambda e: e.tensor_scalar_add(rr[2][:, :], rr[1][:, :], 0.25))
            VEC(lambda e: e.tensor_scalar_add(rr[0][:, :], rr[2][:, :], MAGIC))
            VEC(lambda e: e.tensor_scalar_add(rr[0][:, :], rr[0][:, :], -MAGIC))
            t_r8 = VEC(lambda e: e.tensor_sub(rr[2][:, :], rr[2][:, :],
                                              rr[0][:, :]))["vV"]
            t_sin = SCA(lambda e, a=sin_view: e.activation(a[:, 1:2, :], rr[2][:, :],
                                                           AF.Sin, scale=TWOPI),
                        [t_r8])["sS"]
            t_sin_prev = t_sin
            pet_free[tt % 2] = DMA("sync", peh_d[tt * 128:(tt + 1) * 128, :],
                                   buf[:, :], [t_sin])
        t_peh = pet_free[(NTT - 1) % 2]

        xt = [sb("xt0", [128, D], F32, pb), sb("xt1", [128, D], F32, pb)]
        xsq = [sb("xsq0", [128, D], F32, pb), sb("xsq1", [128, D], F32, pb)]
        xnf = sb("xnf", [128, D], F32, pb)
        xn2f = sb("xn2f", [128, D], F32, pb)
        xnb = sb("xnb", [128, D], BF16, pb)
        xn2b = sb("xn2b", [128, D], BF16, pb)
        cc = [[sb(f"cc{l}_{i}", [128, 1], F32, pb) for i in range(6)]
              for l in range(2)]
        ln_war = [None, None]
        ln_war2 = [None, None]

        def ln(lvl, inp, outp, t_in):
            c = cc[lvl]
            q = xsq[lvl]
            w2_ = [ln_war2[lvl]] if ln_war2[lvl] else []
            t_sq = SCA(lambda e: e.activation(q[:, :], inp[:, :], AF.Square,
                                              accum_out=c[1][:, :]),
                       [t_in] + w2_)["sS"]
            t_r0 = VEC(lambda e: e.reduce_sum(c[0][:, :], inp[:, :], axis=AX.X),
                       [t_in])["vV"]
            vgap()
            t_mu = VEC(lambda e: e.tensor_scalar_mul(c[0][:, :], c[0][:, :],
                                                     1.0 / D), [t_r0])["vV"]
            VEC(lambda e: e.tensor_scalar_mul(c[1][:, :], c[1][:, :], 1.0 / D),
                [t_sq])
            vgap()
            t_c2 = VEC(lambda e: e.tensor_mul(c[2][:, :], c[0][:, :], c[0][:, :]),
                       [t_mu])["vV"]
            vgap()
            t_var = VEC(lambda e: e.tensor_sub(c[1][:, :], c[1][:, :],
                                               c[2][:, :]), [t_c2])["vV"]
            SCA(lambda e: e.activation(c[3][:, :], c[1][:, :], AF.Sqrt,
                                       bias=epsc[:, 0:1]), [t_var])
            t_sd = Sch.tok("sS")
            ln_war2[lvl] = t_sd
            t_rc = VEC(lambda e: e.reciprocal(c[4][:, :], c[3][:, :]), [t_sd])["vV"]
            vgap()
            w_out = [ln_war[lvl]] if ln_war[lvl] else []
            VEC(lambda e: e.tensor_scalar_sub(outp[:, :], inp[:, :], c[0][:, 0:1]),
                [t_mu] + w_out)
            t_o = VEC(lambda e: e.tensor_scalar_mul(outp[:, :], outp[:, :],
                                                    c[4][:, 0:1]), [t_rc])["vV"]
            ln_war[lvl] = None
            return t_o, t_sq

        xt_free = [None, None]
        xnb_free = None
        xn2b_free = None
        prev_sq2 = None
        for tt in range(NTT):
            buf = xt[tt % 2]
            w = list(xt_free[tt % 2]) if xt_free[tt % 2] else []
            t_x = DMA("sync", buf[:, :], x_d[tt * 128:(tt + 1) * 128, :], w)
            ln_war[0] = prev_sq2
            t_xn, t_sq1 = ln(0, buf, xnf, t_x)
            xt_free[tt % 2] = (t_xn, t_sq1)
            t_xn2n, t_sq2 = ln(1, xnf, xn2f, t_xn)
            prev_sq2 = t_sq2
            VEC(lambda e: e.tensor_mul(xn2f[:, :], xn2f[:, :], gbc[:, :]),
                [t_xn2n, t_gbc])
            t_x2 = VEC(lambda e: e.tensor_add(xn2f[:, :], xn2f[:, :], bbc[:, :]),
                       [t_bbc])["vV"]
            wb1 = [xnb_free] if xnb_free else []
            t_c1 = VEC(lambda e: e.tensor_copy(xnb[:, :], xnf[:, :]),
                       [t_xn2n] + wb1)["vV"]
            xnb_free = DMA("sync", xnh_d[tt * 128:(tt + 1) * 128, :], xnb[:, :],
                           [t_c1])
            wb2 = [xn2b_free] if xn2b_free else []
            t_c2 = VEC(lambda e: e.tensor_copy(xn2b[:, :], xn2f[:, :]),
                       [t_x2] + wb2)["vV"]
            xn2b_free = DMA("sync", xn2h_d[tt * 128:(tt + 1) * 128, :], xn2b[:, :],
                            [t_c2])
        t_xnh = xnb_free
        t_xn2h = xn2b_free

        Sch.barrier(exclude=("dG",))
        Sch.flush(nc)
        pb.close()

        pc2 = ExitStack()
        slabX = sb("slabX", [128, 8 * S], BF16, pc2, is_transpose_dest=True)
        slabW = sb("slabW", [128, 8 * 3072], BF16, pc2, is_transpose_dest=True)
        ev = [sb("ev0", [128, S], BF16, pc2), sb("ev1", [128, S], BF16, pc2)]
        vt = [sb("vt0", [128, D], BF16, pc2, is_transpose_dest=True),
              sb("vt1", [128, D], BF16, pc2, is_transpose_dest=True)]

        t_peT = None
        for dc in range(NDC):
            t_peT = DMAT(slabX[:, dc * S:(dc + 1) * S],
                         peh_d[0:S, dc * 128:(dc + 1) * 128], [t_peh])
        t_wposT = None
        for dc in range(NDC):
            t_wposT = DMAT(slabW[:, dc * 2048:(dc + 1) * 2048],
                           wposb_d[0:2048, dc * 128:(dc + 1) * 128], [t_wpos_cast])

        ev_free = [None, None]
        pb_free = [None, None]

        def proj_block(n_feat_tiles, wwid, t_w, t_x, dest_fn):
            for pf in range(n_feat_tiles):
                ebuf = ev[pf % 2]
                efw = [ev_free[pf % 2]] if ev_free[pf % 2] else []
                for tc2 in range(8):
                    pb = pB0a if tc2 % 2 == 0 else pB0b
                    pbw = [pb_free[tc2 % 2]] if pb_free[tc2 % 2] else []
                    for dc in range(NDC):
                        TEN(lambda e, p=pf, t=tc2, d=dc, q=pb:
                            e.matmul(q[:, :],
                                     slabW[:, d * wwid + p * 128:
                                           d * wwid + (p + 1) * 128],
                                     slabX[:, d * S + t * 512:d * S + (t + 1) * 512],
                                     start=(d == 0), stop=(d == NDC - 1)),
                            ([t_w, t_x] + pbw) if dc == 0 else (),
                            inc=(1 if (dc == NDC - 1) else 0))
                    t_mm = Sch.tok("tM")
                    pb_free[tc2 % 2] = VEC(
                        lambda e, t=tc2, b=ebuf, q=pb:
                        e.tensor_copy(b[:, t * 512:(t + 1) * 512], q[:, :]),
                        [t_mm] + (efw if tc2 == 0 else []))["vV"]
                t_ev = Sch.tok("vV")
                ev_free[pf % 2] = dest_fn(pf, ebuf, t_ev)

        def pos_dest(pf, ebuf, t_ev):
            dst = stkK_d if pf < 8 else stkQ_d
            h0 = 2 * (pf % 8)
            DMA("sync", dst[h0, 64:128, :], ebuf[0:64, :], [t_ev])
            return DMA("sync", dst[h0 + 1, 64:128, :], ebuf[64:128, :], [t_ev])

        proj_block(16, 2048, t_wposT, t_peT, pos_dest)

        t_posmm = Sch.tok("tM")
        t_xn2T = None
        for dc in range(NDC):
            t_xn2T = DMAT(slabX[:, dc * S:(dc + 1) * S],
                          xn2h_d[0:S, dc * 128:(dc + 1) * 128],
                          [t_xn2h, t_posmm])
        t_wtokT = None
        for dc in range(NDC):
            t_wtokT = DMAT(slabW[:, dc * 3072:(dc + 1) * 3072],
                           wtokb_d[0:3072, dc * 128:(dc + 1) * 128],
                           [t_wtok_cast, t_posmm])

        def tok_dest(tf, ebuf, t_ev):
            if tf < 16:
                dst = stkK_d if tf < 8 else stkQ_d
                h0 = 2 * (tf % 8)
                DMA("sync", dst[h0, 0:64, :], ebuf[0:64, :], [t_ev])
                return DMA("sync", dst[h0 + 1, 0:64, :], ebuf[64:128, :], [t_ev])
            return DMA("sync", vTh_d[(tf - 16) * 128:(tf - 15) * 128, :],
                       ebuf[:, :], [t_ev])

        proj_block(24, 3072, t_wtokT, t_xn2T, tok_dest)
        t_vTh = ev_free[23 % 2]

        vt_free = [None, None]
        for tt in range(NTT):
            buf = vt[tt % 2]
            w = [t_vTh] + ([vt_free[tt % 2]] if vt_free[tt % 2] else [])
            t_tr = DMAT(buf[:, :], vTh_d[0:D, tt * 128:(tt + 1) * 128], w)
            vt_free[tt % 2] = DMA("sync", vtm_d[tt * 128:(tt + 1) * 128, :],
                                  buf[:, :], [t_tr])
        X["t_vtm"] = vt_free[(NTT - 1) % 2]

        wst = [sb("wst0", [128, F], BF16, pc2, is_transpose_dest=True),
               sb("wst1", [128, F], BF16, pc2, is_transpose_dest=True)]
        wst_free = [None, None]
        repack_done = {}
        for e_i in range(E):
            for dc in range(NDC):
                i = (e_i * NDC + dc) % 2
                w = [X["cast_done"][e_i], X["t_vtm"]] + \
                    ([wst_free[i]] if wst_free[i] else [])
                t_t = DMAT(wst[i][:, :],
                           bass.AP(w1b_d.tensor, e_i * F * D + dc * 128,
                                   [[D, F], [1, 128]]), w)
                wst_free[i] = DMA("scalar", w1tb_d[e_i, dc, :, :], wst[i][:, :],
                                  [t_t])
        wst2 = [sb("wst2a", [128, D], BF16, pc2, is_transpose_dest=True),
                sb("wst2b", [128, D], BF16, pc2, is_transpose_dest=True)]
        wst2_free = [None, None]
        for e_i in range(E):
            for fc in range(32):
                i = (e_i * 32 + fc) % 2
                w = [X["cast_done"][e_i]] + ([wst2_free[i]] if wst2_free[i] else [])
                t_t = DMAT(wst2[i][:, :],
                           bass.AP(w2b_d.tensor, e_i * D * F + fc * 128,
                                   [[F, D], [1, 128]]), w)
                wst2_free[i] = DMA("scalar", w2tb_d[e_i, fc, :, :], wst2[i][:, :],
                                   [t_t])
            repack_done[e_i] = wst2_free[(e_i * 32 + 31) % 2]
        X["repack_done"] = repack_done
        t_repack_all = Sch.tok("dS")



        t_qkvmm = Sch.tok("tM")
        t_xnT = None
        for dc in range(NDC):
            t_xnT = DMAT(slabX[:, dc * S:(dc + 1) * S],
                         xnh_d[0:S, dc * 128:(dc + 1) * 128],
                         [t_xnh, t_qkvmm, t_repack_all])
        Lsb = sb("Lsb", [128, 16], F32, pc2)
        t8 = sb("t8", [128, 8], F32, pc2)
        nv1 = sb("nv1", [128, 1], F32, pc2)
        Ue = sb("Ue", [128, 16], F32, pc2)
        gemask = sb("gemask", [128, 16], F32, pc2)
        rsum = sb("rsum", [128, 1], F32, pc2)
        rrec = sb("rrec", [128, 1], F32, pc2)
        mbf = sb("mbf", [128, 16], BF16, pc2)
        pbl_free = [None, None]
        t_mtprev = t_cbc
        prev_lexp = None
        prev_mtmm = None
        for tt in range(NTT):
            pb = pB0a if tt % 2 == 0 else pB0b
            pbw = [pbl_free[tt % 2]] if pbl_free[tt % 2] else [Sch.tok("vV")]
            for dc in range(NDC):
                TEN(lambda e, t=tt, d=dc, q=pb:
                    e.matmul(q[:, 0:16],
                             slabX[:, d * S + t * 128:d * S + (t + 1) * 128],
                             atT[:, d * 16:(d + 1) * 16],
                             start=(d == 0), stop=(d == NDC - 1)),
                    ([t_xnT, t_atT] + pbw) if dc == 0 else (),
                    inc=(1 if dc == NDC - 1 else 0))
            t_mm = Sch.tok("tM")
            wlex = [prev_lexp] if prev_lexp else []
            vgap(3)
            t_lsb = vlate(lambda e, q=pb: e.tensor_add(Lsb[:, :], q[:, 0:16],
                                                       cbc[:, :]),
                          [t_mm, t_cbc] + wlex)
            pbl_free[tt % 2] = t_lsb
            t_max = vlate(lambda e: e.max(t8[:, :], Lsb[:, :]), [t_lsb])
            t_t8 = vlate(lambda e: e.tensor_scalar_mul(nv1[:, :], t8[:, 0:1],
                                                       -1.0), [t_max])
            t_exp = SCA(lambda e: e.activation(Ue[:, :], Lsb[:, :], AF.Exp,
                                               bias=nv1[:, 0:1]), [t_t8])["sS"]
            prev_lexp = t_exp
            t_gm = vlate(lambda e: e.tensor_scalar(gemask[:, :], Lsb[:, :],
                                                   t8[:, 3:4], None,
                                                   ALU.is_ge), [t_exp, t_max])
            t_ue = vlate(lambda e: e.tensor_mul(Ue[:, :], Ue[:, :], gemask[:, :]),
                         [t_gm])
            t_rs = vlate(lambda e: e.reduce_sum(rsum[:, :], Ue[:, :], axis=AX.X),
                         [t_ue])
            t_rr2 = vlate(lambda e: e.reciprocal(rrec[:, :], rsum[:, :]), [t_rs])
            t_uem = vlate(lambda e: e.tensor_scalar_mul(Ue[:, :], Ue[:, :],
                                                        rrec[:, 0:1]), [t_rr2])
            wmb = [prev_mtmm] if prev_mtmm else []
            t_mb = vlate(lambda e: e.tensor_copy(mbf[:, :], Ue[:, :]),
                         [t_uem] + wmb)
            TEN(lambda e: e.matmul(pC0[0:16, :], mbf[:, :], id128b[:, :],
                                   start=True, stop=True),
                [t_mb, t_id128b, t_mtprev])
            prev_mtmm = Sch.tok("tM")
            vgap(3)
            t_mtprev = vlate(lambda e, t=tt: e.tensor_copy(
                mT[:, t * 128:(t + 1) * 128], pC0[0:16, :]), [prev_mtmm])
        X["t_mT"] = t_mtprev
        X["t_mTh"] = DMA("sync", mTh_d[:, :], mT[:, :], [t_mtprev])

        Sch.barrier(exclude=("dG",))
        Sch.flush(nc)
        pc2.close()

    with ExitStack() as pmn:
        pHa = ps("pHa", [128, 512], pmn)
        pHb = ps("pHb", [128, 512], pmn)
        pFa = ps("pFa", [128, 1024], pmn)
        pFb = ps("pFb", [128, 1024], pmn)
        pSa = ps("pSa", [128, 512], pmn)
        pSb = ps("pSb", [128, 512], pmn)
        pF = [pFa, pFb]
        pFn = ["pFa", "pFb"]
        pHl = [pHa, pHb]
        pHn = ["pHa", "pHb"]
        pSl = [pSa, pSb]
        pSn = ["pSa", "pSb"]

        xnTt = sb("xnTt", [128, 8 * TKT], BF16, pmn, is_transpose_dest=True)
        w1s = sb("w1s", [128, 8192], BF16, pmn, is_transpose_dest=True)
        w2s = sb("w2s", [128, 8192], BF16, pmn, is_transpose_dest=True)
        Hms = sb("Hms", [128, 8192], BF16, pmn)
        slab = sb("slab", [128, 8192], F32, pmn)
        mbc = sb("mbc", [128, TKT], BF16, pmn)
        gts = [sb("gts0", [128, TKT], BF16, pmn)]
        kst = sb("kst", [128, S], BF16, pmn)
        qst = sb("qst", [128, S], BF16, pmn)
        vhb = sb("vh0", [128, 32 * 65], BF16, pmn)
        Tsb = sb("Tsb", [128, WTS], BF16, pmn)
        S1b = [sb("S1a", [128, S], F32, pmn), sb("S1c", [128, S], F32, pmn)]
        Ub = sb("Ub", [128, S], BF16, pmn)
        UTb = [sb("UT0", [128, S], BF16, pmn, is_transpose_dest=True),
               sb("UT1", [128, S], BF16, pmn, is_transpose_dest=True)]
        pvE = [sb("pvE0", [128, 128], BF16, pmn)]
        sTt = [sb("sT0", [128, 128], BF16, pmn, is_transpose_dest=True),
               sb("sT1", [128, 128], BF16, pmn, is_transpose_dest=True)]
        sO2 = sb("sO0", [128, 64], BF16, pmn)
        mrow = sb("mrow", [1, TKT], BF16, pmn)
        mxc = sb("mxc", [128, 1], F32, pmn)
        nmc = sb("nmc", [128, 1], F32, pmn)
        rcf = sb("rcf", [128, 1], F32, pmn)
        rcr = sb("rcr", [128, 1], F32, pmn)

        VEC(lambda e: e.memset(vhb[:, :], 1.0))
        t_init = VEC(lambda e: e.memset(pvE[0][:, :], 0.0))["vV"]

        free = {k: None for k in
                ("xnTt", "w1s", "w2s", "slab", "mbc", "mbc_fill", "mrow",
                 "kst", "qst", "Tsb", "Ub", "xnT_fill",
                 "pHa", "pHb", "pFa", "pFb", "pSa", "pSb")}
        freeL = {"vh": [t_init], "S1": [None, None], "UT": [None, None],
                 "pvE": [None, None], "sT": [None, None], "gts": [None, None],
                 "sO": [None, None]}

        def wf(name):
            t = free[name]
            return [t] if t else []

        fsw = [0]

        ast = {"h": None, "t_k": None, "t_q": None, "t_T": None,
               "prev_aexp": None}
        vtok = {}

        def vload(h, wait):
            vtok[h] = DMA("scalar",
                          vhb[:, :].rearrange("p (tt j) -> p tt j", j=65)[:, :, 0:64],
                          bass.AP(vtm_d.tensor, h * 64,
                                  [[D, 128], [128 * D, 32], [1, 64]]),
                          wait + [X["t_vtm"]])

        def attn_prologue(h):
            ast["h"] = h
            ast["t_k"] = DMA("scalar", kst[:, :], stkK_d[h, :, :], wf("kst"))
            ast["t_q"] = DMA("scalar", qst[:, :], stkQ_d[h, :, :], wf("qst"))
            ast["t_T"] = DMA("scalar", Tsb[:, :],
                             bass.AP(texth_d.tensor, h * 8192, [[1, 128], [1, WTS]]),
                             wf("Tsb") + [X["t_text"]])
            if h == 0:
                vload(0, [Sch.tok("vV")])

        def attn_stage_s(a):
            h, qt = divmod(a, NTT)
            h = h % H
            if qt == 0:
                attn_prologue(h)
            Q0 = qt * 128
            s1 = S1b[a % 2]
            s1w = [freeL["S1"][a % 2]] if freeL["S1"][a % 2] else []
            sb4 = [(pSa, "pSa"), (pSb, "pSb"), (pFa, "pFa"), (pFb, "pFb")]
            for kc in range(8):
                p4, n4 = sb4[kc % 4]
                TEN(lambda e, k=kc, q0=Q0, p=p4:
                    e.matmul(p[:, 0:512], qst[:, q0:q0 + 128],
                             kst[:, k * 512:(k + 1) * 512],
                             start=True, stop=True),
                    [ast["t_k"], ast["t_q"]] + wf(n4))
                t_sc = Sch.tok("tM")
                a0 = Q0 + 3584 - 512 * kc
                free[n4] = VEC(
                    lambda e, k=kc, aa=a0, b=s1, p=p4:
                    e.tensor_add(b[:, k * 512:(k + 1) * 512], p[:, 0:512],
                                 Tsb[:, aa:aa + 512][:, ::-1]),
                    [t_sc, ast["t_T"]] + (s1w if kc == 0 else []))["vV"]
            free["Tsb"] = Sch.tok("vV")
            if qt == NTT - 1:
                free["kst"] = Sch.tok("tM")
                free["qst"] = Sch.tok("tM")
            wax = [ast["prev_aexp"]] if ast["prev_aexp"] else []
            t_mx = vlate(lambda e, b=s1: e.reduce_max(mxc[:, :], b[:, :],
                                                      axis=AX.X), wax)
            t_nm = vlate(lambda e: e.tensor_scalar_mul(nmc[:, :], mxc[:, :],
                                                       -CSC), [t_mx])
            t_exp = SCA(lambda e, b=s1: e.activation(Ub[:, :], b[:, :], AF.Exp,
                                                     bias=nmc[:, 0:1], scale=CSC),
                        [t_nm] + wf("Ub"))["sS"]
            freeL["S1"][a % 2] = t_exp
            ast["prev_aexp"] = t_exp
            ut = UTb[a % 2]
            utw = [freeL["UT"][a % 2]] if freeL["UT"][a % 2] else []
            t_tr = DMAT(ut[:, :].rearrange("p (c f) -> p c f", c=32),
                        Ub[:, :].rearrange("p (c f) -> p c f", c=32),
                        [t_exp] + utw)
            free["Ub"] = t_tr
            return t_tr

        def attn_stage_pv(a, t_tr):
            h, qt = divmod(a, NTT)
            h = h % H
            Q0 = qt * 128
            ut = UTb[a % 2]
            for kc2 in range(32):
                TEN(lambda e, k=kc2, u=ut:
                    e.matmul(pHa[0:65, 0:128], vhb[:, k * 65:(k + 1) * 65],
                             u[:, k * 128:(k + 1) * 128],
                             start=(k == 0), stop=(k == 31)),
                    ([t_tr, vtok[h]] + wf("pHa")) if kc2 == 0 else (),
                    inc=(1 if kc2 == 31 else 0))
            t_pv = Sch.tok("tM")
            freeL["UT"][a % 2] = t_pv
            if qt == NTT - 1 and h + 1 < H:
                vload(h + 1, [t_pv])
            pv = pvE[0]
            pew = [freeL["pvE"][0]] if freeL["pvE"][0] else []
            vgap(3)
            t_ev = VEC(lambda e, b=pv: e.tensor_copy(b[0:65, :],
                                                     pHa[0:65, 0:128]),
                       [t_pv] + pew)["vV"]
            free["pHa"] = t_ev
            st = sTt[a % 2]
            stw = [freeL["sT"][a % 2]] if freeL["sT"][a % 2] else []
            t_str = DMAT(st[:, :], pv[:, :], [t_ev] + stw)
            freeL["pvE"][0] = t_str
            so = sO2
            sow = [freeL["sO"][0]] if freeL["sO"][0] else []
            t_rcf = VEC(lambda e, b=st: e.tensor_copy(rcf[:, :], b[:, 64:65]),
                        [t_str])["vV"]
            vgap()
            t_rc2 = VEC(lambda e: e.reciprocal(rcr[:, :], rcf[:, :]),
                        [t_rcf])["vV"]
            vgap()
            t_div = VEC(lambda e, b=st, o=so: e.tensor_scalar_mul(
                o[:, :], b[:, 0:64], rcr[:, 0:1]), sow + [t_rc2])["vV"]
            freeL["sT"][a % 2] = t_div
            freeL["sO"][0] = DMA(
                "scalar", sh_d[Q0:Q0 + 128, h * 64:(h + 1) * 64],
                so[:, :], [t_div])

        def ffn_unit(T, e_i, q):
            if q == 0:
                t_mr = DMA("scalar", mrow[:, :],
                           mTh_d[e_i:e_i + 1, T * TKT:(T + 1) * TKT],
                           [X["t_mTh"]] + wf("mrow"))
                for t2 in range(2):
                    TEN(lambda e, t=t2:
                        e.matmul(pHa[:, :], onesb[:, :],
                                 mrow[0:1, t * 512:(t + 1) * 512],
                                 start=True, stop=True),
                        [t_mr] + wf("pHa"))
                    t_mbcmm = Sch.tok("tM")
                    vgap(3)
                    free["pHa"] = VEC(
                        lambda e, t=t2: e.tensor_copy(mbc[:, t * 512:(t + 1) * 512],
                                                      pHa[:, :]),
                        [t_mbcmm])["vV"]
                free["mrow"] = Sch.tok("tM")
                free["mbc_fill"] = Sch.tok("vV")
            wex = [X["repack_done"][e_i]] if T == 0 else []
            t_w1 = DMA("sync",
                       w1s[:, :].rearrange("p (b f) -> p b f", b=8),
                       bass.AP(w1tb_d.tensor, e_i * 8 * 128 * F + q * 1024,
                               [[F, 128], [128 * F, 8], [1, 1024]]),
                       wf("w1s") + wex)
            t_w2 = DMA("sync",
                       w2s[:, :].rearrange("p (b f) -> p b f", b=8),
                       bass.AP(w2tb_d.tensor,
                               e_i * 32 * 128 * D + q * 8 * 128 * D,
                               [[D, 128], [128 * D, 8], [1, 1024]]),
                       wf("w2s") + wex)
            gelu_tok = {}
            for t2 in range(2):
                for ft in range(8):
                    hi = (t2 * 8 + ft) % 2
                    for dc in range(NDC):
                        TEN(lambda e, d=dc, f=ft, t=t2, ph=pHl[hi]:
                            e.matmul(ph[:, :],
                                     w1s[:, d * 1024 + f * 128:
                                         d * 1024 + (f + 1) * 128],
                                     xnTt[:, d * TKT + t * 512:
                                          d * TKT + (t + 1) * 512],
                                     start=(d == 0), stop=(d == NDC - 1)),
                            ([t_w1, free["xnT_fill"]] + wf(pHn[hi]))
                            if dc == 0 else (),
                            inc=(1 if dc == NDC - 1 else 0))
                    t_mm = Sch.tok("tM")
                    SCA(lambda e, f=ft, t=t2, ph=pHl[hi], ei=e_i, qq=q:
                        e.activation(Hms[:, f * 1024 + t * 512:
                                         f * 1024 + (t + 1) * 512],
                                     ph[:, :], AF.Gelu,
                                     bias=b1T[:, (qq * 8 + f) * 16 + ei:
                                              (qq * 8 + f) * 16 + ei + 1]),
                        [t_mm, X["t_b1T"]])
                    free[pHn[hi]] = Sch.tok("sS")
                    gelu_tok[(t2, ft)] = Sch.tok("sS")
            free["w1s"] = Sch.tok("tM")
            for ft in range(8):
                VEC(lambda e, f=ft: e.tensor_mul(
                    Hms[:, f * 1024:(f + 1) * 1024],
                    Hms[:, f * 1024:(f + 1) * 1024], mbc[:, :]),
                    [gelu_tok[(1, ft)], gelu_tok[(0, ft)], free["mbc_fill"]])
            t_mask = Sch.tok("vV")
            free["mbc"] = t_mask
            return t_mask, t_w2

        def ffn_unit_m2(T, e_i, q, t_mask, t_w2):
            for t2 in range(2):
                for dtp in range(4):
                    fi = fsw[0] % 2
                    fsw[0] += 1
                    ph = pF[fi]
                    w0 = [t_mask, t_w2] + wf(pFn[fi])
                    nmm = 0
                    for ddt in range(2):
                        for fc in range(8):
                            TEN(lambda e, dd=ddt, f=fc, t=t2, dp=dtp, p=ph:
                                e.matmul(p[:, dd * 512:(dd + 1) * 512],
                                         w2s[:, f * 1024 + (dp * 2 + dd) * 128:
                                             f * 1024 + (dp * 2 + dd) * 128 + 128],
                                         Hms[:, f * 1024 + t * 512:
                                             f * 1024 + (t + 1) * 512],
                                         start=(f == 0), stop=(f == 7)),
                                w0 if nmm == 0 else (),
                                inc=(1 if (ddt == 1 and fc == 7) else 0))
                            nmm += 1
                    t_m2 = Sch.tok("tM")
                    sl_ap = slab[:, :].rearrange("p (dt t) -> p dt t", t=TKT)[
                        :, dtp * 2:dtp * 2 + 2, t2 * 512:(t2 + 1) * 512]
                    pf_ap = ph[:, :].rearrange("p (dd t) -> p dd t", t=512)
                    if e_i == 0 and q == 0:
                        free[pFn[fi]] = VEC(
                            lambda e, o=sl_ap, i=pf_ap: e.tensor_copy(o, i),
                            [t_m2] + wf("slab"))["vV"]
                    else:
                        free[pFn[fi]] = VEC(
                            lambda e, o=sl_ap, i=pf_ap: e.tensor_add(o, o, i),
                            [t_m2])["vV"]
            free["w2s"] = Sch.tok("tM")

        pv_pend = []
        for T in range(NTOUT if stages == 'all' else 0):
            t_xnT = None
            for dc in range(NDC):
                t_xnT = DMAT(xnTt[:, dc * TKT:(dc + 1) * TKT],
                             xnh_d[T * TKT:(T + 1) * TKT, dc * 128:(dc + 1) * 128],
                             wf("xnTt"))
            free["xnT_fill"] = t_xnT
            for u in range(64):
                e_i, q = divmod(u, 4)
                t_mask, t_w2 = ffn_unit(T, e_i, q)
                for j in range(2):
                    a = T * 128 + u * 2 + j
                    t_tr = attn_stage_s(a)
                    pv_pend.append((a, t_tr))
                ffn_unit_m2(T, e_i, q, t_mask, t_w2)
                while pv_pend:
                    attn_stage_pv(*pv_pend.pop(0))
            for dt in range(8):
                for t2 in range(2):
                    fi = fsw[0] % 2
                    fsw[0] += 1
                    ph = pF[fi]
                    TEN(lambda e, d=dt, t=t2, TT=T, p=ph:
                        e.matmul(p[:, 0:512], b2b[:, d * 128:(d + 1) * 128],
                                 mT[:, TT * TKT + t * 512:TT * TKT + (t + 1) * 512],
                                 start=True, stop=True),
                        [X["t_mT"], X["t_b2"]] + wf(pFn[fi]))
                    t_b2m = Sch.tok("tM")
                    vgap(3)
                    free[pFn[fi]] = VEC(
                        lambda e, d=dt, t=t2, p=ph: e.tensor_add(
                            slab[:, d * TKT + t * 512:d * TKT + (t + 1) * 512],
                            slab[:, d * TKT + t * 512:d * TKT + (t + 1) * 512],
                            p[:, 0:512]),
                        [t_b2m])["vV"]
            while pv_pend:
                attn_stage_pv(*pv_pend.pop(0))
            t_slab = Sch.tok("vV")
            for dt in range(8):
                gbuf = gts[0]
                w = [t_slab] + ([freeL["gts"][0]] if freeL["gts"][0] else [])
                t_si = SCA(lambda e, d=dt, b=gbuf:
                           e.activation(b[:, :], slab[:, d * TKT:(d + 1) * TKT],
                                        AF.Silu), w)["sS"]
                freeL["gts"][0] = DMA(
                    "sync", gTh_d[dt * 128:(dt + 1) * 128, T * TKT:(T + 1) * TKT],
                    gbuf[:, :], [t_si])
            free["slab"] = Sch.tok("sS")
            free["xnTt"] = Sch.tok("tM")

        Sch.barrier()
        Sch.flush(nc)

    with ExitStack() as pcm:
        xt2 = [sb("cx0", [128, D], F32, pcm), sb("cx1", [128, D], F32, pcm)]
        st2 = [sb("cs0", [128, D], BF16, pcm), sb("cs1", [128, D], BF16, pcm)]
        gt2 = [sb("cg0", [128, D], BF16, pcm, is_transpose_dest=True),
               sb("cg1", [128, D], BF16, pcm, is_transpose_dest=True)]
        tm2 = [sb("ct0", [128, D], F32, pcm), sb("ct1", [128, D], F32, pcm)]
        frees = [None, None]
        for tt in range(NTT if stages == 'all' else 0):
            i = tt % 2
            w = [frees[i]] if frees[i] else []
            t_x = DMA("scalar", xt2[i][:, :], x_d[tt * 128:(tt + 1) * 128, :], w)
            t_s = DMA("scalar", st2[i][:, :], sh_d[tt * 128:(tt + 1) * 128, :], w)
            t_g = DMAT(gt2[i][:, :], gTh_d[0:D, tt * 128:(tt + 1) * 128], w)
            VEC(lambda e, ii=i: e.tensor_mul(tm2[ii][:, :], st2[ii][:, :],
                                             gt2[ii][:, :]), [t_x, t_s, t_g])
            t_o = VEC(lambda e, ii=i: e.tensor_add(tm2[ii][:, :], tm2[ii][:, :],
                                                   xt2[ii][:, :]))["vV"]
            frees[i] = DMA("sync", y_d[tt * 128:(tt + 1) * 128, :], tm2[i][:, :],
                           [t_o])

        if debug:
            t_all = [Sch.tok(s) for s in ("dS", "dA", "dG", "tM", "vV", "sS")]
            for src, dst in [(xnh_d, dbg["dbg_xn"]), (xn2h_d, dbg["dbg_xn2"]),
                             (peh_d, dbg["dbg_pe"]), (vtm_d, dbg["dbg_v"]),
                             (texth_d, dbg["dbg_text"]), (sh_d, dbg["dbg_s"]),
                             (gTh_d, dbg["dbg_g"])]:
                DMA("gpsimd", dst[:, :], src[:, :], t_all)
            DMA("gpsimd", dbg["dbg_stkK"][:, :, :], stkK_d[:, :, :], t_all)
            DMA("gpsimd", dbg["dbg_stkQ"][:, :, :], stkQ_d[:, :, :], t_all)
            DMA("gpsimd", dbg["dbg_mT"][:, :], mT[:, :], t_all)

        final_waits = [Sch.tok(s) for s in ("dS", "dA", "dG")]
        Sch.do("sync", lambda e: e.nop(), final_waits, ())
        Sch.flush(nc)

    ctx.close()
    return nc


_NC_CACHE = {}


def kernel(L0_attn_x, L0_attn_gamma, L0_attn_beta, L0_attn_w_pos, L0_attn_w_tok,
           L0_attn_bias_table, L0_ffn_keys, L0_ffn_w1, L0_ffn_b1, L0_ffn_w2,
           L0_ffn_b2, L0_ffn_rw, L0_ffn_rb):
    B = L0_attn_x.shape[0]
    key = ("nc", DEBUG, STAGES)
    if key not in _NC_CACHE:
        _NC_CACHE[key] = build_nc(debug=DEBUG, stages=STAGES)
    nc = _NC_CACHE[key]

    f32 = np.float32
    common = dict(
        gamma=np.ascontiguousarray(np.asarray(L0_attn_gamma, f32).reshape(1, D)),
        beta=np.ascontiguousarray(np.asarray(L0_attn_beta, f32).reshape(1, D)),
        w_pos=np.ascontiguousarray(np.asarray(L0_attn_w_pos, f32)),
        w_tok=np.ascontiguousarray(np.asarray(L0_attn_w_tok, f32)),
        bias_table=np.ascontiguousarray(np.asarray(L0_attn_bias_table, f32)),
        keys=np.ascontiguousarray(np.asarray(L0_ffn_keys, f32)),
        rw=np.ascontiguousarray(np.asarray(L0_ffn_rw, f32)),
        rb=np.ascontiguousarray(np.asarray(L0_ffn_rb, f32).reshape(E, 1)),
        w1=np.ascontiguousarray(np.asarray(L0_ffn_w1, f32)),
        b1=np.ascontiguousarray(np.asarray(L0_ffn_b1, f32)),
        w2=np.ascontiguousarray(np.asarray(L0_ffn_w2, f32)),
        b2=np.ascontiguousarray(np.asarray(L0_ffn_b2, f32)),
        id16f=np.eye(16, dtype=f32),
        id16b=np.eye(16, dtype=f32).astype(ml_dtypes.bfloat16),
        id128f=np.eye(128, dtype=f32),
        id128b=np.eye(128, dtype=f32).astype(ml_dtypes.bfloat16),
    )
    xs = np.ascontiguousarray(np.asarray(L0_attn_x, f32))
    in_maps = [dict(common, x=np.ascontiguousarray(xs[b])) for b in range(B)]
    res = run_bass_kernel_spmd(nc, in_maps, core_ids=list(range(B)))
    out = np.stack([np.asarray(r["y"], f32) for r in res.results], axis=0)
    if DEBUG:
        kernel._dbg = res.results
    return out
